# version 31
# baseline (speedup 1.0000x reference)
import contextlib
import os
import numpy as np
import concourse.bass as bass
import concourse.mybir as mybir
from concourse.bass_utils import run_bass_kernel_spmd
from concourse.alu_op_type import AluOpType as OP

F32 = mybir.dt.float32
BF16 = mybir.dt.bfloat16
AF = mybir.ActivationFunctionType
AX = mybir.AxisListType

D = 1024
SEQ = 2048
NKC = 8
NTT = 4
TN = 512
NTI = 16
NHC = 96
NG = 12
GS = 8
NE = 8
GROUP_EXPERT = [0, 1, 2, 3, 4, 4, 5, 5, 6, 6, 7, 7]

ENGS = ("tensor", "vector", "scalar", "gpsimd", "sync")
SELFSYNC = {"vector", "scalar", "gpsimd"}


class Sched:

    def __init__(self):
        self.ops = []
        self.tok_w = {}
        self.tok_r = {}
        self.last_op = {}
        self.dma_count = {}

    def add(self, eng, fn, reads=(), writes=(), dma_key=None, extra_deps=()):
        deps = {}
        for t in reads:
            for i in self.tok_w.get(t, {}).values():
                deps[i] = True
        for t in writes:
            for i in self.tok_w.get(t, {}).values():
                deps[i] = True
            for i in self.tok_r.get(t, {}).values():
                deps[i] = True
        for d in extra_deps:
            if d is not None:
                deps[d] = True
        opid = len(self.ops)
        dma = None
        if dma_key is not None:
            n = self.dma_count.get(dma_key, 0) + 1
            self.dma_count[dma_key] = n
            dma = (dma_key, 16 * n)
        deps.pop(opid, None)
        self.ops.append(dict(eng=eng, fn=fn, deps=sorted(deps), dma=dma,
                             inc=False, cum=0))
        for t in reads:
            self.tok_r.setdefault(t, {})[eng] = opid
        for t in writes:
            self.tok_w[t] = {eng: opid}
            self.tok_r[t] = {}
        self.last_op[eng] = opid
        return opid

    def barrier(self):
        lasts = dict(self.last_op)
        for e in ENGS:
            dep = [v for k, v in lasts.items() if k != e]
            self.add(e, lambda eng: eng.nop(), extra_deps=dep)

    def analyze(self):
        for o in self.ops:
            for d in o["deps"]:
                p = self.ops[d]
                if p["dma"] is not None:
                    continue
                if p["eng"] == o["eng"] and p["eng"] not in SELFSYNC:
                    continue
                p["inc"] = True
        cum = {e: 0 for e in ENGS}
        for o in self.ops:
            if o["inc"]:
                cum[o["eng"]] += 1
            o["cum"] = cum[o["eng"]]

    def emit(self, block, eng_sems, dma_sems):
        self.analyze()
        streams = {e: [] for e in ENGS}
        for o in self.ops:
            streams[o["eng"]].append(o)

        def mk(ename):
            ops = streams[ename]

            def body(eng):
                lastw = {}
                for o in ops:
                    wants = {}
                    for d in o["deps"]:
                        p = self.ops[d]
                        if p["dma"] is not None:
                            k = ("dma", p["dma"][0])
                            v = p["dma"][1]
                        else:
                            if p["eng"] == ename and ename not in SELFSYNC:
                                continue
                            k = ("eng", p["eng"])
                            v = p["cum"]
                        if wants.get(k, 0) < v:
                            wants[k] = v
                    for k in sorted(wants):
                        v = wants[k]
                        if lastw.get(k, 0) >= v:
                            continue
                        sem = dma_sems[k[1]] if k[0] == "dma" else eng_sems[k[1]]
                        eng.wait_ge(sem, v)
                        lastw[k] = v
                    inst = o["fn"](eng)
                    if o["dma"] is not None:
                        inst.then_inc(dma_sems[o["dma"][0]], 16)
                    elif o["inc"]:
                        inst.then_inc(eng_sems[ename], 1)
            return body

        for e in ENGS:
            getattr(block, e)(mk(e))


def build_nc(stages=3, debug=None):
    nc = bass.Bass("TRN2", target_bir_lowering=False)

    xd = nc.dram_tensor("x", [SEQ, D], F32, kind="ExternalInput").ap()
    outd = nc.dram_tensor("out", [SEQ, D], F32, kind="ExternalOutput").ap()
    w1h = nc.dram_tensor("w1h", [NHC, 128, NKC, 128], F32, kind="ExternalInput").ap()
    w2h = nc.dram_tensor("w2h", [NKC, NG, 128, GS, 128], F32, kind="ExternalInput").ap()
    wih = nc.dram_tensor("wih", [NKC, 128, NKC, 128], F32, kind="ExternalInput").ap()
    wrh = nc.dram_tensor("wrh", [NKC, 128, NKC, 128], F32, kind="ExternalInput").ap()
    wxh = nc.dram_tensor("wxh", [NKC, 128, NKC, 128], F32, kind="ExternalInput").ap()
    woh = nc.dram_tensor("woh", [NKC, 128, NKC, 128], F32, kind="ExternalInput").ap()
    rtd = nc.dram_tensor("rt", [128, NKC, NE], F32, kind="ExternalInput").ap()
    bcol = nc.dram_tensor("bcol", [128, 40], F32, kind="ExternalInput").ap()
    seld = nc.dram_tensor("sel", [NE, NE * 128], F32, kind="ExternalInput").ap()
    onescol = nc.dram_tensor("onescol", [128, 1], F32, kind="ExternalInput").ap()
    onesrow = nc.dram_tensor("onesrow", [1, 128], F32, kind="ExternalInput").ap()
    identd = nc.dram_tensor("ident", [128, 128], F32, kind="ExternalInput").ap()
    epsd = nc.dram_tensor("epsd", [128, 1], F32, kind="ExternalInput").ap()
    dbgd = nc.dram_tensor("dbg", [128, 8192], F32, kind="ExternalOutput").ap()

    sch = Sched()

    with contextlib.ExitStack() as top:
        ent = top.enter_context
        X = ent(nc.sbuf_tensor("X", [128, NKC * SEQ], F32))
        XN = ent(nc.sbuf_tensor("XN", [128, NKC * SEQ], BF16))
        HP = ent(nc.sbuf_tensor("HP", [128, GS * SEQ], BF16))
        SIO = [ent(nc.sbuf_tensor(f"SIO{i}", [128, D], F32)) for i in range(2)]
        stB = ent(nc.sbuf_tensor("stB", [1, TN], F32))
        stC = ent(nc.sbuf_tensor("stC", [1, TN], F32))
        rb0 = ent(nc.sbuf_tensor("rb0", [128, TN], BF16))
        rb1 = ent(nc.sbuf_tensor("rb1", [128, TN], BF16))
        RB = [rb0, rb1]
        xsq = [ent(nc.sbuf_tensor(f"xsq{i}", [128, TN], BF16)) for i in range(2)]
        xbr = [ent(nc.sbuf_tensor(f"xbr{i}", [128, TN], BF16)) for i in range(2)]
        psb = ent(nc.sbuf_tensor("psb", [NE, TN], F32))
        ptw = ent(nc.sbuf_tensor("ptw", [128, 128], F32))
        cwb = ent(nc.sbuf_tensor("cwb", [128, 128], F32))
        m1s = ent(nc.sbuf_tensor("m1s", [128, 16], F32))
        m2s = ent(nc.sbuf_tensor("m2s", [128, 16], F32))
        bia = ent(nc.sbuf_tensor("bia", [128, 40], F32))
        a_sb = ent(nc.sbuf_tensor("a_sb", [128, NE], F32))
        na_sb = ent(nc.sbuf_tensor("na_sb", [128, NE], F32))
        rt_sb = ent(nc.sbuf_tensor("rt_sb", [128, NKC * NE], BF16))
        sel_sb = ent(nc.sbuf_tensor("sel_sb", [NE, NE * 128], BF16))
        cb_sb = ent(nc.sbuf_tensor("cb_sb", [128, SEQ], BF16))
        xnt = [ent(nc.sbuf_tensor(f"xnt{i}", [128, NKC * TN], BF16)) for i in range(2)]
        ct_sb = ent(nc.sbuf_tensor("ct_sb", [NE, SEQ], BF16))
        ocol = ent(nc.sbuf_tensor("ocol", [128, 1], F32))
        ocolb = ent(nc.sbuf_tensor("ocolb", [128, 1], BF16))
        orow = ent(nc.sbuf_tensor("orow", [1, 128], F32))
        orowb = ent(nc.sbuf_tensor("orowb", [1, 128], BF16))
        stMb = ent(nc.sbuf_tensor("stMb", [1, TN], BF16))
        stRb = ent(nc.sbuf_tensor("stRb", [1, TN], BF16))
        idn = ent(nc.sbuf_tensor("idn", [128, 128], F32))
        carry = ent(nc.sbuf_tensor("carry", [128, NKC], F32))
        eps_sb = ent(nc.sbuf_tensor("eps_sb", [128, 1], F32))
        wor = [ent(nc.sbuf_tensor(f"wor{i}", [128, NKC * 128], BF16)) for i in range(2)]
        pw = [[ent(nc.sbuf_tensor(f"pw{j}{i}", [128, NKC * 128], BF16))
               for i in range(2)] for j in range(3)]

        PH = [ent(nc.psum_tensor(f"PH{i}", [128, TN], F32)) for i in range(4)]
        PY = [ent(nc.psum_tensor(f"PY{i}", [128, TN], F32)) for i in range(2)]
        PC = ent(nc.psum_tensor("PC", [128, TN], F32))
        PD = ent(nc.psum_tensor("PD", [128, TN], F32))

        eng_sems = {e: ent(nc.semaphore(f"se_{e}")) for e in ENGS}
        dkeys = (["cbia", "crt", "csel", "cocol", "cocolb", "corow", "cidn",
                  "ceps"] + [f"cdbg{i}" for i in range(8)]
                 + [f"w1s{i}" for i in range(3)]
                 + [f"w2s{i}" for i in range(2)] + [f"sio{i}" for i in range(2)] + ["corowb"]
                 + [f"pw{j}{p}" for j in range(3) for p in range(2)]
                 + [f"wo{p}" for p in range(2)])
        dma_sems = {k: ent(nc.semaphore(f"sd_{k}")) for k in dkeys}

        dbg_state = {"col": 0}

        def dbg_tap(tag, ap, w, tokens):
            if debug != tag:
                return
            c0 = dbg_state["col"]
            dbg_state["col"] += w
            np_ = ap.shape[0]
            sch.add("gpsimd", lambda g, ap=ap, c0=c0, w=w, np_=np_: g.dma_start(
                out=dbgd[0:np_, c0:c0 + w], in_=ap), reads=list(tokens),
                writes=["DBGOUT"], dma_key="cdbg0")

        def xs(dc, tt, w=TN):
            return X[:, dc * SEQ + tt * w: dc * SEQ + (tt + 1) * w]

        def xns(kc, tt, w=TN):
            return XN[:, kc * SEQ + tt * w: kc * SEQ + (tt + 1) * w]

        def hps(lhc, tt):
            return HP[:, lhc * SEQ + tt * TN: lhc * SEQ + (tt + 1) * TN]

        sch.add("gpsimd", lambda g: g.dma_start(out=bia[:], in_=bcol),
                writes=["bia"], dma_key="cbia")
        sch.add("gpsimd", lambda g: g.dma_start(
            out=rt_sb[:].rearrange("p (k e) -> p k e", k=NKC), in_=rtd),
            writes=["rt"], dma_key="crt")
        sch.add("gpsimd", lambda g: g.dma_start(out=sel_sb[:], in_=seld),
                writes=["sel"], dma_key="csel")
        sch.add("gpsimd", lambda g: g.dma_start(out=ocol[:], in_=onescol),
                writes=["ocol"], dma_key="cocol")
        sch.add("gpsimd", lambda g: g.dma_start(out=ocolb[:], in_=onescol),
                writes=["ocolb"], dma_key="cocolb")
        sch.add("gpsimd", lambda g: g.dma_start(out=orow[:], in_=onesrow),
                writes=["orow"], dma_key="corow")
        sch.add("gpsimd", lambda g: g.dma_start(out=orowb[:], in_=onesrow),
                writes=["orowb"], dma_key="corowb")
        sch.add("gpsimd", lambda g: g.dma_start(out=idn[:], in_=identd),
                writes=["idn"], dma_key="cidn")
        sch.add("gpsimd", lambda g: g.dma_start(out=eps_sb[:], in_=epsd),
                writes=["eps"], dma_key="ceps")
        sch.add("scalar", lambda s: s.activation(a_sb[:], bia[:, 32:40], AF.Sigmoid),
                reads=["bia"], writes=["a"])
        sch.add("vector", lambda v: v.tensor_scalar(
            na_sb[:], a_sb[:], -1.0, None, OP.mult, OP.bypass),
            reads=["a"], writes=["na"])


        def emit_ln(call, tt, dest_fn, dest_tok, dest3_fn=None):
            cv = f"c{call}"
            for kc in range(NKC):
                xsl = xs(kc, tt)
                xb = xbr[kc % 2]
                xq = xsq[kc % 2]
                sch.add("scalar", lambda s, xb=xb, xsl=xsl: s.copy(xb[:], xsl),
                        reads=[f"X:{kc}:{tt}"], writes=[f"xbr{kc % 2}"])
                sch.add("vector", lambda v, xb=xb, xq=xq: v.tensor_tensor(
                    xq[:], xb[:], xb[:], OP.mult),
                    reads=[f"xbr{kc % 2}"], writes=[f"xsq{kc % 2}"])
                sch.add("tensor", lambda t, kc=kc, xb=xb: t.matmul(
                    PC[0:1, :], ocolb[:], xb[:], start=(kc == 0), stop=(kc == NKC - 1)),
                    reads=[f"xbr{kc % 2}", "ocolb"], writes=["PC"])
                sch.add("tensor", lambda t, kc=kc, xq=xq: t.matmul(
                    PD[0:1, :], ocolb[:], xq[:], start=(kc == 0), stop=(kc == NKC - 1)),
                    reads=[f"xsq{kc % 2}", "ocolb"], writes=["PD"])
            sch.add("vector", lambda v: v.tensor_scalar(
                stMb[:], PC[0:1, :], 1.0 / D, None, OP.mult, OP.bypass),
                reads=["PC"], writes=["stMb"])
            sch.add("vector", lambda v: v.tensor_scalar(
                stB[:], PD[0:1, :], 1.0 / D, None, OP.mult, OP.bypass),
                reads=["PD"], writes=["stB"])
            sch.add("vector", lambda v: v.tensor_tensor(stC[:], stMb[:], stMb[:], OP.mult),
                    reads=["stMb"], writes=["stC"])
            sch.add("vector", lambda v: v.tensor_tensor(stRb[:], stB[:], stC[:], OP.subtract),
                    reads=["stB", "stC"], writes=["stRb"])
            sch.add("tensor", lambda t: t.matmul(PC[:], orowb[:], stMb[:], start=True, stop=True),
                    reads=["stMb", "orowb", "PC"], writes=["PC"])
            sch.add("tensor", lambda t: t.matmul(PD[:], orowb[:], stRb[:], start=True, stop=True),
                    reads=["stRb", "orowb", "PD"], writes=["PD"])
            sch.add("scalar", lambda s: s.activation(PD[:], PD[:], AF.Ln,
                                                     bias=eps_sb[:, 0:1]),
                    reads=["PD", "eps"], writes=["PD"])
            rbp = RB[tt % 2]
            sch.add("scalar", lambda s, rbp=rbp: s.activation(rbp[:], PD[:], AF.Exp, scale=-0.5),
                    reads=["PD"], writes=[f"rb{tt % 2}"])
            dst3 = dest3_fn()
            x3 = X[:].rearrange("p (k s) -> p k s", k=NKC)[:, :, tt * TN:(tt + 1) * TN]
            mb3 = PC[:].rearrange("p (k s) -> p k s", k=1)
            rb3 = rbp[:].rearrange("p (k s) -> p k s", k=1)
            dtoks = [dest_fn(kc)[1] for kc in range(NKC)]
            xtoks = [f"X:{kc}:{tt}" for kc in range(NKC)]
            b_x, b_m = bass.broadcast_tensor_aps(x3, mb3)
            sch.add("vector", lambda v, dst3=dst3, b_x=b_x, b_m=b_m: v.tensor_tensor(
                dst3, b_x, b_m, OP.subtract),
                reads=xtoks + ["PC"], writes=dtoks)
            d3b, b_r = bass.broadcast_tensor_aps(dst3, rb3)
            sch.add("vector", lambda v, dst3=dst3, d3b=d3b, b_r=b_r: v.tensor_tensor(
                dst3, d3b, b_r, OP.mult),
                reads=dtoks + [f"rb{tt % 2}"], writes=dtoks)

        def emit_router_tt(call, tt):
            cv = f"c{call}"
            for kc in range(NKC):
                sch.add("tensor", lambda t, kc=kc, tt=tt: t.matmul(
                    PY[0][0:NE, :], rt_sb[:, kc * NE:(kc + 1) * NE], xns(kc, tt),
                    start=(kc == 0), stop=(kc == NKC - 1)),
                    reads=[f"XN:{kc}:{tt}:{cv}", "rt"], writes=["PY0"])
            sch.add("scalar", lambda s: s.activation(psb[:], PY[0][0:NE, :], AF.Sigmoid),
                    reads=["PY0"], writes=["psb"])
            for c4 in range(4):
                c = tt * 4 + c4
                sch.add("tensor", lambda t, c4=c4, c=c: t.transpose(
                    PH[1][:, c * 8:(c + 1) * 8], psb[:, c4 * 128:(c4 + 1) * 128], idn[0:NE, 0:NE]),
                    reads=["psb", "idn"], writes=["PH1"])

        def emit_router_fin(call):
            cv = f"c{call}"
            sch.add("vector", lambda v: v.tensor_copy(ptw[:], PH[1][:, 0:128]),
                    reads=["PH1"], writes=["ptw"])
            pt3 = ptw[:].rearrange("p (c e) -> p c e", e=8)
            cw3 = cwb[:].rearrange("p (c e) -> p c e", e=8)
            m13 = m1s[:].rearrange("p (c e) -> p c e", e=1)
            m23 = m2s[:].rearrange("p (c e) -> p c e", e=1)
            sch.add("vector", lambda v: v.tensor_reduce(m1s[:], pt3, AX.X, OP.max),
                    reads=["ptw"], writes=["m1"])
            b_p, b_m1 = bass.broadcast_tensor_aps(pt3, m13)
            sch.add("vector", lambda v: v.tensor_tensor(cw3, b_p, b_m1, OP.is_lt),
                    reads=["ptw", "m1"], writes=["cw"])
            sch.add("vector", lambda v: v.tensor_tensor(cw3, pt3, cw3, OP.mult),
                    reads=["ptw", "cw"], writes=["cw"])
            sch.add("vector", lambda v: v.tensor_reduce(m2s[:], cw3, AX.X, OP.max),
                    reads=["cw"], writes=["m2"])
            b_p2, b_m2 = bass.broadcast_tensor_aps(pt3, m23)
            sch.add("vector", lambda v: v.tensor_tensor(cw3, b_p2, b_m2, OP.is_ge),
                    reads=["ptw", "m2"], writes=["cw"])
            sch.add("vector", lambda v: v.tensor_tensor(m1s[:], m1s[:], m2s[:], OP.add),
                    reads=["m1", "m2"], writes=["m1"])
            sch.add("vector", lambda v: v.tensor_scalar(
                m1s[:], m1s[:], 2.0, 2e-20, OP.mult, OP.add),
                reads=["m1"], writes=["m1"])
            sch.add("scalar", lambda s: s.activation(m2s[:], m1s[:], AF.Ln),
                    reads=["m1"], writes=["m2"])
            sch.add("scalar", lambda s: s.activation(m2s[:], m2s[:], AF.Exp, scale=-1.0),
                    reads=["m2"], writes=["m2"])
            sch.add("vector", lambda v: v.tensor_tensor(cw3, cw3, pt3, OP.mult),
                    reads=["cw", "ptw"], writes=["cw"])
            b_c, b_r = bass.broadcast_tensor_aps(cw3, m23)
            sch.add("vector", lambda v: v.tensor_tensor(cw3, b_c, b_r, OP.mult),
                    reads=["cw", "m2"], writes=["cw"])
            for grp in range(4):
                for c4 in range(4):
                    c = grp * 4 + c4
                    sch.add("tensor", lambda t, c4=c4, c=c: t.transpose(
                        PC[0:NE, c4 * 128:(c4 + 1) * 128], cwb[:, c * 8:(c + 1) * 8], idn[:]),
                        reads=["cw", "idn"], writes=["PC"])
                sch.add("scalar", lambda s, grp=grp: s.copy(
                    ct_sb[:, grp * TN:(grp + 1) * TN], PC[0:NE, :]),
                    reads=["PC"], writes=[f"ct:{grp}"])

        def emit_router(call):
            for tt in range(NTT):
                emit_router_tt(call, tt)
            emit_router_fin(call)

        def emit_cb(call, e):
            cv = f"c{call}"
            for tt in range(NTT):
                sch.add("tensor", lambda t, e=e, tt=tt: t.matmul(
                    PH[0][:], sel_sb[:, e * 128:(e + 1) * 128],
                    ct_sb[0:NE, tt * TN:(tt + 1) * TN], start=True, stop=True),
                    reads=[f"ct:{tt}", "sel"], writes=["PH0"])
                sch.add("scalar", lambda s, tt=tt: s.copy(
                    cb_sb[:, tt * TN:(tt + 1) * TN], PH[0][:]),
                    reads=["PH0"], writes=[f"cb:{tt}"])

        def emit_ffn_group(call, G):
            cv = f"c{call}"
            e = GROUP_EXPERT[G]
            for lhc in range(GS):
                hc = G * GS + lhc
                slot = hc % 3
                sch.add("gpsimd", lambda g, hc=hc, slot=slot: g.dma_start(
                    out=w1r[slot][:].rearrange("p (k m) -> p k m", k=NKC),
                    in_=w1h[hc]),
                    writes=[f"w1r{slot}"], dma_key=f"w1s{slot}")
                BK6 = [PH[0], PH[1], PH[2], PH[3], PY[0], PY[1]]
                BT6 = ["PH0", "PH1", "PH2", "PH3", "PY0", "PY1"]
                bidx = lambda tt: (hc * NTT + tt) % 6
                for kc in range(NKC):
                    for tt in range(NTT):
                        hb = BK6[bidx(tt)]
                        sch.add("tensor", lambda t, slot=slot, kc=kc, tt=tt, hb=hb: t.matmul(
                            hb[:], w1r[slot][:, kc * 128:(kc + 1) * 128],
                            xns(kc, tt), start=(kc == 0), stop=(kc == NKC - 1)),
                            reads=[f"w1r{slot}", f"XN:{kc}:{tt}:{cv}"],
                            writes=[BT6[bidx(tt)]])
                for tt in range(NTT):
                    rp = (lhc * NTT + tt) % 2
                    hb = BK6[bidx(tt)]
                    sch.add("scalar", lambda s, rp=rp, hb=hb: s.activation(
                        rsc[rp][:], hb[:], AF.Relu),
                        reads=[BT6[bidx(tt)]], writes=[f"rsc{rp}"])
                    if call == 1 and G == 0 and lhc == 0 and tt == 0:
                        dbg_tap("rh", rsc[rp][:], TN, [f"rsc{rp}"])
                    sch.add("vector", lambda v, rp=rp, lhc=lhc, tt=tt: v.tensor_tensor(
                        hps(lhc, tt), rsc[rp][:], rsc[rp][:], OP.mult),
                        reads=[f"rsc{rp}"], writes=[f"HP:{lhc}:{tt}"])
                    sch.add("vector", lambda v, lhc=lhc, tt=tt: v.tensor_tensor(
                        hps(lhc, tt), hps(lhc, tt), cb_sb[:, tt * TN:(tt + 1) * TN], OP.mult),
                        reads=[f"cb:{tt}", f"HP:{lhc}:{tt}"], writes=[f"HP:{lhc}:{tt}"])
            YB8 = [PH[0], PH[1], PH[2], PH[3], PY[0], PY[1], PC, PD]
            YT8 = ["PH0", "PH1", "PH2", "PH3", "PY0", "PY1", "PC", "PD"]
            for dc in range(NKC):
                slot = (G * NKC + dc) % 2
                sch.add("gpsimd", lambda g, dc=dc, slot=slot, G=G: g.dma_start(
                    out=w2r[slot][:].rearrange("p (h m) -> p h m", h=GS),
                    in_=w2h[dc, G]),
                    writes=[f"w2r{slot}"], dma_key=f"w2s{slot}")
                for ttg in range(2):
                    tts = [2 * ttg, 2 * ttg + 1]
                    for lhc in range(GS):
                        for tt in tts:
                            yb = YB8[(dc * 4 + tt) % 8]
                            sch.add("tensor", lambda t, slot=slot, lhc=lhc, tt=tt, yb=yb: t.matmul(
                                yb[:], w2r[slot][:, lhc * 128:(lhc + 1) * 128],
                                hps(lhc, tt), start=(lhc == 0), stop=(lhc == GS - 1)),
                                reads=[f"w2r{slot}", f"HP:{lhc}:{tt}"],
                                writes=[YT8[(dc * 4 + tt) % 8]])
                    for tt in tts:
                        yb = YB8[(dc * 4 + tt) % 8]
                        sch.add("vector", lambda v, dc=dc, tt=tt, yb=yb: v.tensor_tensor(
                            xs(dc, tt), xs(dc, tt), yb[:], OP.add),
                            reads=[f"X:{dc}:{tt}", YT8[(dc * 4 + tt) % 8]],
                            writes=[f"X:{dc}:{tt}"])

        def emit_ffn_call(call, with_ln_router=True):
            cv = f"c{call}"
            if with_ln_router:
                for tt in range(NTT):
                    emit_ln(call, tt,
                            lambda kc, tt=tt: (xns(kc, tt), f"XN:{kc}:{tt}:{cv}"),
                            cv,
                            dest3_fn=lambda tt=tt: XN[:].rearrange(
                                "p (k s) -> p k s", k=NKC)[:, :, tt * TN:(tt + 1) * TN])
            if call == 1:
                dbg_tap("xn", XN[:, 0:SEQ], SEQ,
                        [f"XN:0:{t}:{cv}" for t in range(NTT)])
                dbg_tap("xn", XN[:, SEQ:2 * SEQ], SEQ,
                        [f"XN:1:{t}:{cv}" for t in range(NTT)])
            if with_ln_router:
                emit_router(call)
            if call == 1:
                dbg_tap("ct", ct_sb[0:NE, :], SEQ, [f"ct:{t}" for t in range(4)])
            done_e = set()
            for G in range(NG):
                e = GROUP_EXPERT[G]
                if e not in done_e:
                    done_e.add(e)
                    emit_cb(call, e)
                    if call == 1 and e == 0:
                        dbg_tap("cb", cb_sb[:, :], SEQ, [f"cb:{t}" for t in range(NTT)])
                emit_ffn_group(call, G)
                if call == 1 and G == 0:
                    dbg_tap("hp", HP[:, 0:SEQ], SEQ,
                            [f"HP:0:{t}" for t in range(NTT)])
                    dbg_tap("hp", HP[:, SEQ:2 * SEQ], SEQ,
                            [f"HP:1:{t}" for t in range(NTT)])

        def xnts(par, kc):
            return xnt[par][:, kc * TN:(kc + 1) * TN]

        def emit_ln2(tt):
            par = tt % 2
            emit_ln(2, tt,
                    lambda kc, par=par: (xnts(par, kc), f"XT:{par}:{kc}"),
                    f"a{tt}",
                    dest3_fn=lambda par=par: xnt[par][:].rearrange(
                        "p (k s) -> p k s", k=NKC))

        for ti in range(NTI):
            p = ti % 2
            sch.add("sync", lambda sy, p=p, ti=ti: sy.dma_start(
                out=SIO[p][:], in_=xd[ti * 128:(ti + 1) * 128, :]),
                writes=[f"SIO{p}"], dma_key=f"sio{p}")
            for dc in range(NKC):
                pt = PH[2] if dc < 4 else PH[3]
                ptok = "PH2" if dc < 4 else "PH3"
                sch.add("tensor", lambda t, p=p, dc=dc, pt=pt: t.transpose(
                    pt[:, (dc % 4) * 128:(dc % 4 + 1) * 128],
                    SIO[p][:, dc * 128:(dc + 1) * 128], idn[:]),
                    reads=[f"SIO{p}", "idn"], writes=[ptok])
            for dc in range(NKC):
                pt = PH[2] if dc < 4 else PH[3]
                ptok = "PH2" if dc < 4 else "PH3"
                ev = "scalar" if dc < 4 else "vector"
                dst = X[:, dc * SEQ + ti * 128: dc * SEQ + (ti + 1) * 128]
                src = pt[:, (dc % 4) * 128:(dc % 4 + 1) * 128]
                if ev == "scalar":
                    sch.add("scalar", lambda s, dst=dst, src=src: s.copy(dst, src),
                            reads=[ptok], writes=[f"X:{dc}:{ti // 4}"])
                else:
                    sch.add("vector", lambda v, dst=dst, src=src: v.tensor_copy(dst, src),
                            reads=[ptok], writes=[f"X:{dc}:{ti // 4}"])
        for tt in range(NTT):
            emit_ln(1, tt,
                    lambda kc, tt=tt: (xns(kc, tt), f"XN:{kc}:{tt}:c1"),
                    "c1",
                    dest3_fn=lambda tt=tt: XN[:].rearrange(
                        "p (k s) -> p k s", k=NKC)[:, :, tt * TN:(tt + 1) * TN])
            emit_router_tt(1, tt)
        emit_router_fin(1)

        with contextlib.ExitStack() as ffn1_scope:
            fent = ffn1_scope.enter_context
            w1r = [fent(nc.sbuf_tensor(f"w1r{i}_a", [128, NKC * 128], BF16)) for i in range(3)]
            w2r = [fent(nc.sbuf_tensor(f"w2r{i}_a", [128, GS * 128], BF16)) for i in range(2)]
            rsc = [fent(nc.sbuf_tensor(f"rsc{i}_a", [128, TN], BF16)) for i in range(2)]
            emit_ffn_call(1)

        if stages >= 2:
            emit_ln2(0)
            emit_ln2(1)
            for oc in (0, 1):
                for j, wsrc in enumerate([wih, wrh, wxh]):
                    sch.add("gpsimd", lambda g, j=j, oc=oc, wsrc=wsrc: g.dma_start(
                        out=pw[j][oc % 2][:].rearrange("p (k m) -> p k m", k=NKC),
                        in_=wsrc[oc]),
                        writes=[f"pw{j}{oc % 2}"], dma_key=f"pw{j}{oc % 2}")
            sch.barrier()
            emit_attn_sublayer = True
        else:
            emit_attn_sublayer = False
        with contextlib.ExitStack() as attn_scope:
            aent = attn_scope.enter_context
            Atp = [XN[:, 0:NKC * TN], XN[:, NKC * TN:2 * NKC * TN]]
            Btp = [HP[:, 0:NKC * TN], HP[:, NKC * TN:2 * NKC * TN]]
            Htp = [HP[:, 2 * NKC * TN:3 * NKC * TN], HP[:, 3 * NKC * TN:4 * NKC * TN]]
            srs = [aent(nc.sbuf_tensor("srs0", [128, TN], F32))]
            sis = [aent(nc.sbuf_tensor("sis0", [128, TN], F32))]
            sin_ = [aent(nc.sbuf_tensor("sinb0", [128, TN], F32))]
            WSRC = [wih, wrh, wxh]

            def emit_proj(tt, scan_tt=None, wout_tt=None, skip_dma_ocs=()):
                par = tt % 2
                for oc in range(NKC):
                    wpar = oc % 2
                    if oc not in skip_dma_ocs:
                        for j in range(3):
                            sch.add("gpsimd", lambda g, j=j, wpar=wpar, oc=oc: g.dma_start(
                                out=pw[j][wpar][:].rearrange("p (k m) -> p k m", k=NKC),
                                in_=WSRC[j][oc]),
                                writes=[f"pw{j}{wpar}"], dma_key=f"pw{j}{wpar}")
                    banks = [PH[0], PH[1], PH[2]] if oc % 2 == 0 else [PH[3], PY[0], PY[1]]
                    btok = ["PH0", "PH1", "PH2"] if oc % 2 == 0 else ["PH3", "PY0", "PY1"]
                    for j in range(3):
                        for kc in range(NKC):
                            sch.add("tensor", lambda t, j=j, kc=kc, wpar=wpar, par=par, banks=banks: t.matmul(
                                banks[j][:], pw[j][wpar][:, kc * 128:(kc + 1) * 128],
                                xnts(par, kc), start=(kc == 0), stop=(kc == NKC - 1)),
                                reads=[f"pw{j}{wpar}", f"XT:{par}:{kc}"],
                                writes=[btok[j]])
                    sp = 0
                    sch.add("scalar", lambda s, sp=sp, banks=banks, oc=oc: s.activation(
                        sin_[sp][:], banks[0][:], AF.Identity, bias=bia[:, 0 + oc:1 + oc]),
                        reads=[btok[0], "bia"], writes=[f"sin{sp}"])
                    sch.add("scalar", lambda s, sp=sp, banks=banks, oc=oc: s.activation(
                        srs[sp][:], banks[1][:], AF.Sigmoid, bias=bia[:, 8 + oc:9 + oc]),
                        reads=[btok[1], "bia"], writes=[f"srs{sp}"])
                    sch.add("scalar", lambda s, sp=sp, banks=banks, oc=oc: s.activation(
                        sis[sp][:], banks[2][:], AF.Sigmoid, bias=bia[:, 16 + oc:17 + oc]),
                        reads=[btok[2], "bia"], writes=[f"sis{sp}"])
                    asl = Atp[par][:, oc * TN:(oc + 1) * TN]
                    bsl = Btp[par][:, oc * TN:(oc + 1) * TN]
                    sch.add("vector", lambda v, sp=sp, asl=asl, oc=oc: v.tensor_scalar(
                        asl, srs[sp][:], a_sb[:, oc:oc + 1], None, OP.mult, OP.bypass),
                        reads=[f"srs{sp}", "a"], writes=[f"A:{par}:{oc}"])
                    sch.add("vector", lambda v, sp=sp, oc=oc: v.tensor_scalar(
                        srs[sp][:], srs[sp][:], na_sb[:, oc:oc + 1], 1.0, OP.mult, OP.add),
                        reads=[f"srs{sp}", "na"], writes=[f"srs{sp}"])
                    sch.add("vector", lambda v, sp=sp: v.tensor_tensor(
                        sis[sp][:], sis[sp][:], sin_[sp][:], OP.mult),
                        reads=[f"sis{sp}", f"sin{sp}"], writes=[f"sis{sp}"])
                    sch.add("vector", lambda v, sp=sp, bsl=bsl, oc=oc: v.tensor_tensor(
                        bsl, srs[sp][:], sis[sp][:], OP.mult),
                        reads=[f"srs{sp}", f"sis{sp}"], writes=[f"B:{par}:{oc}"])
                    if scan_tt is not None and oc < 4:
                        emit_scan(scan_tt, only_fc=2 * oc)
                        emit_scan(scan_tt, only_fc=2 * oc + 1)
                    if wout_tt is not None and oc >= 4:
                        emit_wout_chain(wout_tt, 2 * (oc - 4))
                        emit_wout_chain(wout_tt, 2 * (oc - 4) + 1)

            def emit_scan(tt, only_fc=None):
                par = tt % 2
                fcs = range(NKC) if only_fc is None else [only_fc]
                for fc in fcs:
                    hsl = Htp[par][:, fc * TN:(fc + 1) * TN]
                    asl = Atp[par][:, fc * TN:(fc + 1) * TN]
                    bsl = Btp[par][:, fc * TN:(fc + 1) * TN]
                    init = 0.0 if tt == 0 else carry[:, fc:fc + 1]
                    sch.add("vector", lambda v, hsl=hsl, asl=asl, bsl=bsl, init=init: v.tensor_tensor_scan(
                        hsl, asl, bsl, init, OP.mult, OP.add),
                        reads=[f"A:{par}:{fc}", f"B:{par}:{fc}"] + ([] if tt == 0 else [f"cr:{fc}"]),
                        writes=[f"H:{par}:{fc}"])
                    sch.add("vector", lambda v, fc=fc, par=par: v.tensor_copy(
                        carry[:, fc:fc + 1], Htp[par][:, fc * TN + TN - 1: fc * TN + TN]),
                        reads=[f"H:{par}:{fc}"], writes=[f"cr:{fc}"])

            def emit_wout_chain(tt, dc, ybanks=None):
                wpar = dc % 2
                sch.add("gpsimd", lambda g, dc=dc, wpar=wpar: g.dma_start(
                    out=wor[wpar][:].rearrange("p (k m) -> p k m", k=NKC),
                    in_=woh[dc]),
                    writes=[f"wor{wpar}"], dma_key=f"wo{wpar}")
                if ybanks is None:
                    pb = PC if dc % 2 == 0 else PD
                    pbt = "PC" if dc % 2 == 0 else "PD"
                else:
                    pb, pbt = ybanks[dc % len(ybanks)]
                par = tt % 2
                for ec in range(NKC):
                    sch.add("tensor", lambda t, wpar=wpar, ec=ec, pb=pb, par=par: t.matmul(
                        pb[:], wor[wpar][:, ec * 128:(ec + 1) * 128],
                        Htp[par][:, ec * TN:(ec + 1) * TN], start=(ec == 0), stop=(ec == NKC - 1)),
                        reads=[f"wor{wpar}", f"H:{par}:{ec}"], writes=[pbt])
                sch.add("vector", lambda v, dc=dc, tt=tt, pb=pb: v.scalar_tensor_tensor(
                    xs(dc, tt), pb[:], bia[:, 24 + dc:25 + dc], xs(dc, tt),
                    OP.add, OP.add),
                    reads=[pbt, "bia", f"X:{dc}:{tt}"], writes=[f"X:{dc}:{tt}"])

            def emit_wout(tt):
                for dc in range(NKC):
                    emit_wout_chain(tt, dc)

            if emit_attn_sublayer:
                emit_proj(0, skip_dma_ocs=(0, 1))
                for tt in range(NTT):
                    if tt + 2 < NTT:
                        emit_ln2(tt + 2)
                    if tt + 1 < NTT:
                        emit_proj(tt + 1, scan_tt=tt, wout_tt=tt)
                    elif stages == 2:
                        emit_scan(tt)
                        emit_wout(tt)

        if stages >= 3:
            sch.barrier()
            emit_scan(3)
            WB = [(PH[0], "PH0"), (PH[1], "PH1"), (PH[2], "PH2"), (PH[3], "PH3")]

            def _ln3(tt):
                emit_ln(3, tt,
                        lambda kc, tt=tt: (xns(kc, tt), f"XN:{kc}:{tt}:c3"),
                        "c3",
                        dest3_fn=lambda tt=tt: XN[:].rearrange(
                            "p (k s) -> p k s", k=NKC)[:, :, tt * TN:(tt + 1) * TN])
            _ln3(0)
            for dc in (0, 1, 2):
                emit_wout_chain(3, dc, ybanks=WB)
            _ln3(1)
            for dc in (3, 4, 5):
                emit_wout_chain(3, dc, ybanks=WB)
            _ln3(2)
            for dc in (6, 7):
                emit_wout_chain(3, dc, ybanks=WB)
            _ln3(3)
            emit_router(3)

        with contextlib.ExitStack() as ffn2_scope:
            fent = ffn2_scope.enter_context
            w1r = [fent(nc.sbuf_tensor(f"w1r{i}_b", [128, NKC * 128], BF16)) for i in range(3)]
            w2r = [fent(nc.sbuf_tensor(f"w2r{i}_b", [128, GS * 128], BF16)) for i in range(2)]
            rsc = [fent(nc.sbuf_tensor(f"rsc{i}_b", [128, TN], BF16)) for i in range(2)]
            if stages >= 3:
                emit_ffn_call(3, with_ln_router=False)

        for ti in range(NTI):
            p = ti % 2
            for dc in range(NKC):
                pt = PC if dc < 4 else PD
                ptok = "PC" if dc < 4 else "PD"
                src = X[:, dc * SEQ + ti * 128: dc * SEQ + (ti + 1) * 128]
                sch.add("tensor", lambda t, pt=pt, src=src, dc=dc: t.transpose(
                    pt[:, (dc % 4) * 128:(dc % 4 + 1) * 128], src, idn[:]),
                    reads=[f"X:{dc}:{ti // 4}", "idn"], writes=[ptok])
            for half, (pt, ptok) in enumerate([(PC, "PC"), (PD, "PD")]):
                dst = SIO[p][:, half * TN:(half + 1) * TN]
                sch.add("scalar", lambda s, dst=dst, pt=pt: s.copy(dst, pt[:]),
                        reads=[ptok],
                        writes=[f"SIO{p}"])
            sch.add("sync", lambda sy, p=p, ti=ti: sy.dma_start(
                out=outd[ti * 128:(ti + 1) * 128, :], in_=SIO[p][:]),
                reads=[f"SIO{p}"], writes=[f"OD:{ti}"], dma_key=f"sio{p}")
        sch.add("sync", lambda sy: sy.nop(),
                extra_deps=[sch.tok_w[f"OD:{ti}"]["sync"] for ti in range(NTI)],
                reads=[f"OD:{ti}" for ti in range(NTI)]
                + (["DBGOUT"] if debug else []))

        with nc.Block() as block:
            sch.emit(block, eng_sems, dma_sems)

    return nc


_NC_CACHE = None
_LAST_RES = None


def kernel(L0_attn_x, L0_attn_W_in, L0_attn_b_in, L0_attn_W_r, L0_attn_b_r,
           L0_attn_W_i, L0_attn_b_i, L0_attn_W_out, L0_attn_b_out, L0_attn_log_a,
           L0_ffn_router_w, L0_ffn_w1, L0_ffn_w2):
    x = np.ascontiguousarray(np.asarray(L0_attn_x, np.float32))

    def prep_sq(W):
        W = np.asarray(W, np.float32)
        return np.ascontiguousarray(W.reshape(8, 128, 8, 128).transpose(2, 1, 0, 3))

    w1hh = np.ascontiguousarray(
        np.asarray(L0_ffn_w1, np.float32).reshape(8, 128, 96, 128).transpose(2, 1, 0, 3))
    w2hh = np.ascontiguousarray(
        np.asarray(L0_ffn_w2, np.float32).reshape(12, 8, 128, 8, 128).transpose(3, 0, 2, 1, 4))
    wih = prep_sq(L0_attn_W_in)
    wrh = prep_sq(L0_attn_W_r)
    wxh = prep_sq(L0_attn_W_i)
    woh = prep_sq(L0_attn_W_out)
    rt = np.ascontiguousarray(
        np.asarray(L0_ffn_router_w, np.float32).T.reshape(8, 128, 8).transpose(1, 0, 2))

    def col(v):
        return np.asarray(v, np.float32).reshape(8, 128).T

    bcol = np.ascontiguousarray(np.concatenate(
        [col(L0_attn_b_in), col(L0_attn_b_r), col(L0_attn_b_i),
         col(L0_attn_b_out), col(L0_attn_log_a)], axis=1))
    sel = np.zeros((8, 1024), np.float32)
    for e in range(8):
        sel[e, e * 128:(e + 1) * 128] = 1.0
    onescol = np.ones((128, 1), np.float32)
    onesrow = np.ones((1, 128), np.float32)
    ident = np.eye(128, dtype=np.float32)
    epsv = np.full((128, 1), 1e-5, np.float32)

    stages = int(os.environ.get("MK_STAGES", "3"))
    dbgtag = os.environ.get("MK_DEBUG") or None
    global _NC_CACHE
    key = (stages, dbgtag)
    if _NC_CACHE is None or _NC_CACHE[0] != key:
        _NC_CACHE = (key, build_nc(stages, dbgtag))
    nc = _NC_CACHE[1]

    shared = {"w1h": w1hh, "w2h": w2hh, "wih": wih, "wrh": wrh, "wxh": wxh,
              "woh": woh, "rt": rt, "bcol": bcol, "sel": sel,
              "onescol": onescol, "onesrow": onesrow, "ident": ident, "epsd": epsv}
    in_maps = [dict(shared, x=np.ascontiguousarray(x[c])) for c in range(8)]
    res = run_bass_kernel_spmd(nc, in_maps, core_ids=list(range(8)))
    global _LAST_RES
    _LAST_RES = res
    out = np.stack([np.asarray(r["out"]) for r in res.results], axis=0)
    return out


# version 32
# speedup vs baseline: 1.0033x; 1.0033x over previous
import contextlib
import os
import numpy as np
import concourse.bass as bass
import concourse.mybir as mybir
from concourse.bass_utils import run_bass_kernel_spmd
from concourse.alu_op_type import AluOpType as OP

F32 = mybir.dt.float32
BF16 = mybir.dt.bfloat16
AF = mybir.ActivationFunctionType
AX = mybir.AxisListType

D = 1024
SEQ = 2048
NKC = 8
NTT = 4
TN = 512
NTI = 16
NHC = 96
NG = 12
GS = 8
NE = 8
GROUP_EXPERT = [0, 1, 2, 3, 4, 4, 5, 5, 6, 6, 7, 7]

ENGS = ("tensor", "vector", "scalar", "gpsimd", "sync")
SELFSYNC = {"vector", "scalar", "gpsimd"}


class Sched:

    def __init__(self):
        self.ops = []
        self.tok_w = {}
        self.tok_r = {}
        self.last_op = {}
        self.dma_count = {}

    def add(self, eng, fn, reads=(), writes=(), dma_key=None, extra_deps=()):
        deps = {}
        for t in reads:
            for i in self.tok_w.get(t, {}).values():
                deps[i] = True
        for t in writes:
            for i in self.tok_w.get(t, {}).values():
                deps[i] = True
            for i in self.tok_r.get(t, {}).values():
                deps[i] = True
        for d in extra_deps:
            if d is not None:
                deps[d] = True
        opid = len(self.ops)
        dma = None
        if dma_key is not None:
            n = self.dma_count.get(dma_key, 0) + 1
            self.dma_count[dma_key] = n
            dma = (dma_key, 16 * n)
        deps.pop(opid, None)
        self.ops.append(dict(eng=eng, fn=fn, deps=sorted(deps), dma=dma,
                             inc=False, cum=0))
        for t in reads:
            self.tok_r.setdefault(t, {})[eng] = opid
        for t in writes:
            self.tok_w[t] = {eng: opid}
            self.tok_r[t] = {}
        self.last_op[eng] = opid
        return opid

    def barrier(self):
        lasts = dict(self.last_op)
        for e in ENGS:
            dep = [v for k, v in lasts.items() if k != e]
            self.add(e, lambda eng: eng.nop(), extra_deps=dep)

    def analyze(self):
        for o in self.ops:
            for d in o["deps"]:
                p = self.ops[d]
                if p["dma"] is not None:
                    continue
                if p["eng"] == o["eng"] and p["eng"] not in SELFSYNC:
                    continue
                p["inc"] = True
        cum = {e: 0 for e in ENGS}
        for o in self.ops:
            if o["inc"]:
                cum[o["eng"]] += 1
            o["cum"] = cum[o["eng"]]

    def emit(self, block, eng_sems, dma_sems):
        self.analyze()
        streams = {e: [] for e in ENGS}
        for o in self.ops:
            streams[o["eng"]].append(o)

        def mk(ename):
            ops = streams[ename]

            def body(eng):
                lastw = {}
                for o in ops:
                    wants = {}
                    for d in o["deps"]:
                        p = self.ops[d]
                        if p["dma"] is not None:
                            k = ("dma", p["dma"][0])
                            v = p["dma"][1]
                        else:
                            if p["eng"] == ename and ename not in SELFSYNC:
                                continue
                            k = ("eng", p["eng"])
                            v = p["cum"]
                        if wants.get(k, 0) < v:
                            wants[k] = v
                    for k in sorted(wants):
                        v = wants[k]
                        if lastw.get(k, 0) >= v:
                            continue
                        sem = dma_sems[k[1]] if k[0] == "dma" else eng_sems[k[1]]
                        eng.wait_ge(sem, v)
                        lastw[k] = v
                    inst = o["fn"](eng)
                    if o["dma"] is not None:
                        inst.then_inc(dma_sems[o["dma"][0]], 16)
                    elif o["inc"]:
                        inst.then_inc(eng_sems[ename], 1)
            return body

        for e in ENGS:
            getattr(block, e)(mk(e))


def build_nc(stages=3, debug=None):
    nc = bass.Bass("TRN2", target_bir_lowering=False)

    xd = nc.dram_tensor("x", [SEQ, D], F32, kind="ExternalInput").ap()
    outd = nc.dram_tensor("out", [SEQ, D], F32, kind="ExternalOutput").ap()
    w1h = nc.dram_tensor("w1h", [NHC, 128, NKC, 128], F32, kind="ExternalInput").ap()
    w2h = nc.dram_tensor("w2h", [NKC, NG, 128, GS, 128], F32, kind="ExternalInput").ap()
    wih = nc.dram_tensor("wih", [NKC, 128, NKC, 128], F32, kind="ExternalInput").ap()
    wrh = nc.dram_tensor("wrh", [NKC, 128, NKC, 128], F32, kind="ExternalInput").ap()
    wxh = nc.dram_tensor("wxh", [NKC, 128, NKC, 128], F32, kind="ExternalInput").ap()
    woh = nc.dram_tensor("woh", [NKC, 128, NKC, 128], F32, kind="ExternalInput").ap()
    rtd = nc.dram_tensor("rt", [128, NKC, NE], F32, kind="ExternalInput").ap()
    bcol = nc.dram_tensor("bcol", [128, 40], F32, kind="ExternalInput").ap()
    seld = nc.dram_tensor("sel", [NE, NE * 128], F32, kind="ExternalInput").ap()
    onescol = nc.dram_tensor("onescol", [128, 1], F32, kind="ExternalInput").ap()
    onesrow = nc.dram_tensor("onesrow", [1, 128], F32, kind="ExternalInput").ap()
    identd = nc.dram_tensor("ident", [128, 128], F32, kind="ExternalInput").ap()
    epsd = nc.dram_tensor("epsd", [128, 1], F32, kind="ExternalInput").ap()
    dbgd = nc.dram_tensor("dbg", [128, 8192], F32, kind="ExternalOutput").ap()

    sch = Sched()

    with contextlib.ExitStack() as top:
        ent = top.enter_context
        X = ent(nc.sbuf_tensor("X", [128, NKC * SEQ], F32))
        XN = ent(nc.sbuf_tensor("XN", [128, NKC * SEQ], BF16))
        HP = ent(nc.sbuf_tensor("HP", [128, GS * SEQ], BF16))
        SIO = [ent(nc.sbuf_tensor(f"SIO{i}", [128, D], F32)) for i in range(3)]
        stA = ent(nc.sbuf_tensor("stA", [1, TN], F32))
        stB = ent(nc.sbuf_tensor("stB", [1, TN], F32))
        stC = ent(nc.sbuf_tensor("stC", [1, TN], F32))
        rb0 = ent(nc.sbuf_tensor("rb0", [128, TN], BF16))
        rb1 = ent(nc.sbuf_tensor("rb1", [128, TN], BF16))
        RB = [rb0, rb1]
        xsq = [ent(nc.sbuf_tensor(f"xsq{i}", [128, TN], BF16)) for i in range(2)]
        xbr = [ent(nc.sbuf_tensor(f"xbr{i}", [128, TN], BF16)) for i in range(2)]
        psb = ent(nc.sbuf_tensor("psb", [NE, TN], F32))
        ptw = ent(nc.sbuf_tensor("ptw", [128, 128], F32))
        cwb = ent(nc.sbuf_tensor("cwb", [128, 128], F32))
        m1s = ent(nc.sbuf_tensor("m1s", [128, 16], F32))
        m2s = ent(nc.sbuf_tensor("m2s", [128, 16], F32))
        bia = ent(nc.sbuf_tensor("bia", [128, 40], F32))
        a_sb = ent(nc.sbuf_tensor("a_sb", [128, NE], F32))
        na_sb = ent(nc.sbuf_tensor("na_sb", [128, NE], F32))
        rt_sb = ent(nc.sbuf_tensor("rt_sb", [128, NKC * NE], BF16))
        sel_sb = ent(nc.sbuf_tensor("sel_sb", [NE, NE * 128], BF16))
        cb_sb = ent(nc.sbuf_tensor("cb_sb", [128, SEQ], BF16))
        xnt = [ent(nc.sbuf_tensor(f"xnt{i}", [128, NKC * TN], BF16)) for i in range(2)]
        ct_sb = ent(nc.sbuf_tensor("ct_sb", [NE, SEQ], BF16))
        ocol = ent(nc.sbuf_tensor("ocol", [128, 1], F32))
        ocolb = ent(nc.sbuf_tensor("ocolb", [128, 1], BF16))
        orow = ent(nc.sbuf_tensor("orow", [1, 128], F32))
        orowb = ent(nc.sbuf_tensor("orowb", [1, 128], BF16))
        stMb = ent(nc.sbuf_tensor("stMb", [1, TN], BF16))
        stRb = ent(nc.sbuf_tensor("stRb", [1, TN], BF16))
        idn = ent(nc.sbuf_tensor("idn", [128, 128], F32))
        carry = ent(nc.sbuf_tensor("carry", [128, NKC], F32))
        eps_sb = ent(nc.sbuf_tensor("eps_sb", [128, 1], F32))
        wor = [ent(nc.sbuf_tensor(f"wor{i}", [128, NKC * 128], BF16)) for i in range(2)]

        PH = [ent(nc.psum_tensor(f"PH{i}", [128, TN], F32)) for i in range(4)]
        PY = [ent(nc.psum_tensor(f"PY{i}", [128, TN], F32)) for i in range(2)]
        PC = ent(nc.psum_tensor("PC", [128, TN], F32))
        PD = ent(nc.psum_tensor("PD", [128, TN], F32))

        eng_sems = {e: ent(nc.semaphore(f"se_{e}")) for e in ENGS}
        dkeys = (["cbia", "crt", "csel", "cocol", "cocolb", "corow", "cidn",
                  "ceps"] + [f"cdbg{i}" for i in range(8)]
                 + [f"w1s{i}" for i in range(4)]
                 + [f"w2s{i}" for i in range(2)] + [f"sio{i}" for i in range(3)] + ["corowb"]
                 + [f"pw{j}{p}" for j in range(3) for p in range(2)]
                 + [f"wo{p}" for p in range(2)])
        dma_sems = {k: ent(nc.semaphore(f"sd_{k}")) for k in dkeys}

        dbg_state = {"col": 0}

        def dbg_tap(tag, ap, w, tokens):
            if debug != tag:
                return
            c0 = dbg_state["col"]
            dbg_state["col"] += w
            np_ = ap.shape[0]
            sch.add("gpsimd", lambda g, ap=ap, c0=c0, w=w, np_=np_: g.dma_start(
                out=dbgd[0:np_, c0:c0 + w], in_=ap), reads=list(tokens),
                writes=["DBGOUT"], dma_key="cdbg0")

        def xs(dc, tt, w=TN):
            return X[:, dc * SEQ + tt * w: dc * SEQ + (tt + 1) * w]

        def xns(kc, tt, w=TN):
            return XN[:, kc * SEQ + tt * w: kc * SEQ + (tt + 1) * w]

        def hps(lhc, tt):
            return HP[:, lhc * SEQ + tt * TN: lhc * SEQ + (tt + 1) * TN]

        sch.add("gpsimd", lambda g: g.dma_start(out=bia[:], in_=bcol),
                writes=["bia"], dma_key="cbia")
        sch.add("gpsimd", lambda g: g.dma_start(
            out=rt_sb[:].rearrange("p (k e) -> p k e", k=NKC), in_=rtd),
            writes=["rt"], dma_key="crt")
        sch.add("gpsimd", lambda g: g.dma_start(out=sel_sb[:], in_=seld),
                writes=["sel"], dma_key="csel")
        sch.add("gpsimd", lambda g: g.dma_start(out=ocol[:], in_=onescol),
                writes=["ocol"], dma_key="cocol")
        sch.add("gpsimd", lambda g: g.dma_start(out=ocolb[:], in_=onescol),
                writes=["ocolb"], dma_key="cocolb")
        sch.add("gpsimd", lambda g: g.dma_start(out=orow[:], in_=onesrow),
                writes=["orow"], dma_key="corow")
        sch.add("gpsimd", lambda g: g.dma_start(out=orowb[:], in_=onesrow),
                writes=["orowb"], dma_key="corowb")
        sch.add("gpsimd", lambda g: g.dma_start(out=idn[:], in_=identd),
                writes=["idn"], dma_key="cidn")
        sch.add("gpsimd", lambda g: g.dma_start(out=eps_sb[:], in_=epsd),
                writes=["eps"], dma_key="ceps")
        sch.add("scalar", lambda s: s.activation(a_sb[:], bia[:, 32:40], AF.Sigmoid),
                reads=["bia"], writes=["a"])
        sch.add("vector", lambda v: v.tensor_scalar(
            na_sb[:], a_sb[:], -1.0, None, OP.mult, OP.bypass),
            reads=["a"], writes=["na"])


        def emit_ln(call, tt, dest_fn, dest_tok, dest3_fn=None):
            cv = f"c{call}"
            for kc in range(NKC):
                xsl = xs(kc, tt)
                xb = xbr[kc % 2]
                xq = xsq[kc % 2]
                sch.add("scalar", lambda s, xb=xb, xsl=xsl: s.copy(xb[:], xsl),
                        reads=[f"X:{kc}:{tt}"], writes=[f"xbr{kc % 2}"])
                sch.add("vector", lambda v, xb=xb, xq=xq: v.tensor_tensor(
                    xq[:], xb[:], xb[:], OP.mult),
                    reads=[f"xbr{kc % 2}"], writes=[f"xsq{kc % 2}"])
                sch.add("tensor", lambda t, kc=kc, xb=xb: t.matmul(
                    PC[0:1, :], ocolb[:], xb[:], start=(kc == 0), stop=(kc == NKC - 1)),
                    reads=[f"xbr{kc % 2}", "ocolb"], writes=["PC"])
                sch.add("tensor", lambda t, kc=kc, xq=xq: t.matmul(
                    PD[0:1, :], ocolb[:], xq[:], start=(kc == 0), stop=(kc == NKC - 1)),
                    reads=[f"xsq{kc % 2}", "ocolb"], writes=["PD"])
            sch.add("vector", lambda v: v.tensor_scalar(
                stMb[:], PC[0:1, :], 1.0 / D, None, OP.mult, OP.bypass),
                reads=["PC"], writes=["stMb"])
            sch.add("vector", lambda v: v.tensor_scalar(
                stB[:], PD[0:1, :], 1.0 / D, None, OP.mult, OP.bypass),
                reads=["PD"], writes=["stB"])
            sch.add("vector", lambda v: v.tensor_tensor(stC[:], stMb[:], stMb[:], OP.mult),
                    reads=["stMb"], writes=["stC"])
            sch.add("vector", lambda v: v.tensor_tensor(stRb[:], stB[:], stC[:], OP.subtract),
                    reads=["stB", "stC"], writes=["stRb"])
            sch.add("tensor", lambda t: t.matmul(PC[:], orowb[:], stMb[:], start=True, stop=True),
                    reads=["stMb", "orowb", "PC"], writes=["PC"])
            sch.add("tensor", lambda t: t.matmul(PD[:], orowb[:], stRb[:], start=True, stop=True),
                    reads=["stRb", "orowb", "PD"], writes=["PD"])
            sch.add("scalar", lambda s: s.activation(PD[:], PD[:], AF.Ln,
                                                     bias=eps_sb[:, 0:1]),
                    reads=["PD", "eps"], writes=["PD"])
            rbp = RB[tt % 2]
            sch.add("scalar", lambda s, rbp=rbp: s.activation(rbp[:], PD[:], AF.Exp, scale=-0.5),
                    reads=["PD"], writes=[f"rb{tt % 2}"])
            dst3 = dest3_fn()
            x3 = X[:].rearrange("p (k s) -> p k s", k=NKC)[:, :, tt * TN:(tt + 1) * TN]
            mb3 = PC[:].rearrange("p (k s) -> p k s", k=1)
            rb3 = rbp[:].rearrange("p (k s) -> p k s", k=1)
            dtoks = [dest_fn(kc)[1] for kc in range(NKC)]
            xtoks = [f"X:{kc}:{tt}" for kc in range(NKC)]
            b_x, b_m = bass.broadcast_tensor_aps(x3, mb3)
            sch.add("vector", lambda v, dst3=dst3, b_x=b_x, b_m=b_m: v.tensor_tensor(
                dst3, b_x, b_m, OP.subtract),
                reads=xtoks + ["PC"], writes=dtoks)
            d3b, b_r = bass.broadcast_tensor_aps(dst3, rb3)
            sch.add("vector", lambda v, dst3=dst3, d3b=d3b, b_r=b_r: v.tensor_tensor(
                dst3, d3b, b_r, OP.mult),
                reads=dtoks + [f"rb{tt % 2}"], writes=dtoks)

        def emit_router_tt(call, tt):
            cv = f"c{call}"
            for kc in range(NKC):
                sch.add("tensor", lambda t, kc=kc, tt=tt: t.matmul(
                    PY[0][0:NE, :], rt_sb[:, kc * NE:(kc + 1) * NE], xns(kc, tt),
                    start=(kc == 0), stop=(kc == NKC - 1)),
                    reads=[f"XN:{kc}:{tt}:{cv}", "rt"], writes=["PY0"])
            sch.add("scalar", lambda s: s.activation(psb[:], PY[0][0:NE, :], AF.Sigmoid),
                    reads=["PY0"], writes=["psb"])
            for c4 in range(4):
                c = tt * 4 + c4
                sch.add("tensor", lambda t, c4=c4, c=c: t.transpose(
                    PH[1][:, c * 8:(c + 1) * 8], psb[:, c4 * 128:(c4 + 1) * 128], idn[0:NE, 0:NE]),
                    reads=["psb", "idn"], writes=["PH1"])

        def emit_router_fin(call):
            cv = f"c{call}"
            sch.add("vector", lambda v: v.tensor_copy(ptw[:], PH[1][:, 0:128]),
                    reads=["PH1"], writes=["ptw"])
            pt3 = ptw[:].rearrange("p (c e) -> p c e", e=8)
            cw3 = cwb[:].rearrange("p (c e) -> p c e", e=8)
            m13 = m1s[:].rearrange("p (c e) -> p c e", e=1)
            m23 = m2s[:].rearrange("p (c e) -> p c e", e=1)
            sch.add("vector", lambda v: v.tensor_reduce(m1s[:], pt3, AX.X, OP.max),
                    reads=["ptw"], writes=["m1"])
            b_p, b_m1 = bass.broadcast_tensor_aps(pt3, m13)
            sch.add("vector", lambda v: v.tensor_tensor(cw3, b_p, b_m1, OP.is_lt),
                    reads=["ptw", "m1"], writes=["cw"])
            sch.add("vector", lambda v: v.tensor_tensor(cw3, pt3, cw3, OP.mult),
                    reads=["ptw", "cw"], writes=["cw"])
            sch.add("vector", lambda v: v.tensor_reduce(m2s[:], cw3, AX.X, OP.max),
                    reads=["cw"], writes=["m2"])
            b_p2, b_m2 = bass.broadcast_tensor_aps(pt3, m23)
            sch.add("vector", lambda v: v.tensor_tensor(cw3, b_p2, b_m2, OP.is_ge),
                    reads=["ptw", "m2"], writes=["cw"])
            sch.add("vector", lambda v: v.tensor_tensor(m1s[:], m1s[:], m2s[:], OP.add),
                    reads=["m1", "m2"], writes=["m1"])
            sch.add("vector", lambda v: v.tensor_scalar(
                m1s[:], m1s[:], 2.0, 2e-20, OP.mult, OP.add),
                reads=["m1"], writes=["m1"])
            sch.add("scalar", lambda s: s.activation(m2s[:], m1s[:], AF.Ln),
                    reads=["m1"], writes=["m2"])
            sch.add("scalar", lambda s: s.activation(m2s[:], m2s[:], AF.Exp, scale=-1.0),
                    reads=["m2"], writes=["m2"])
            sch.add("vector", lambda v: v.tensor_tensor(cw3, cw3, pt3, OP.mult),
                    reads=["cw", "ptw"], writes=["cw"])
            b_c, b_r = bass.broadcast_tensor_aps(cw3, m23)
            sch.add("vector", lambda v: v.tensor_tensor(cw3, b_c, b_r, OP.mult),
                    reads=["cw", "m2"], writes=["cw"])
            for grp in range(4):
                for c4 in range(4):
                    c = grp * 4 + c4
                    sch.add("tensor", lambda t, c4=c4, c=c: t.transpose(
                        PC[0:NE, c4 * 128:(c4 + 1) * 128], cwb[:, c * 8:(c + 1) * 8], idn[:]),
                        reads=["cw", "idn"], writes=["PC"])
                sch.add("scalar", lambda s, grp=grp: s.copy(
                    ct_sb[:, grp * TN:(grp + 1) * TN], PC[0:NE, :]),
                    reads=["PC"], writes=[f"ct:{grp}"])

        def emit_router(call):
            for tt in range(NTT):
                emit_router_tt(call, tt)
            emit_router_fin(call)

        def emit_cb(call, e):
            cv = f"c{call}"
            for tt in range(NTT):
                sch.add("tensor", lambda t, e=e, tt=tt: t.matmul(
                    PH[0][:], sel_sb[:, e * 128:(e + 1) * 128],
                    ct_sb[0:NE, tt * TN:(tt + 1) * TN], start=True, stop=True),
                    reads=[f"ct:{tt}", "sel"], writes=["PH0"])
                sch.add("scalar", lambda s, tt=tt: s.copy(
                    cb_sb[:, tt * TN:(tt + 1) * TN], PH[0][:]),
                    reads=["PH0"], writes=[f"cb:{tt}"])

        def emit_ffn_group(call, G):
            cv = f"c{call}"
            e = GROUP_EXPERT[G]
            for lhc in range(GS):
                hc = G * GS + lhc
                slot = hc % 4
                sch.add("gpsimd", lambda g, hc=hc, slot=slot: g.dma_start(
                    out=w1r[slot][:].rearrange("p (k m) -> p k m", k=NKC),
                    in_=w1h[hc]),
                    writes=[f"w1r{slot}"], dma_key=f"w1s{slot}")
                BK6 = [PH[0], PH[1], PH[2], PH[3], PY[0], PY[1]]
                BT6 = ["PH0", "PH1", "PH2", "PH3", "PY0", "PY1"]
                bidx = lambda tt: (hc * NTT + tt) % 6
                for kc in range(NKC):
                    for tt in range(NTT):
                        hb = BK6[bidx(tt)]
                        sch.add("tensor", lambda t, slot=slot, kc=kc, tt=tt, hb=hb: t.matmul(
                            hb[:], w1r[slot][:, kc * 128:(kc + 1) * 128],
                            xns(kc, tt), start=(kc == 0), stop=(kc == NKC - 1)),
                            reads=[f"w1r{slot}", f"XN:{kc}:{tt}:{cv}"],
                            writes=[BT6[bidx(tt)]])
                for tt in range(NTT):
                    rp = (lhc * NTT + tt) % 3
                    hb = BK6[bidx(tt)]
                    sch.add("scalar", lambda s, rp=rp, hb=hb: s.activation(
                        rsc[rp][:], hb[:], AF.Relu),
                        reads=[BT6[bidx(tt)]], writes=[f"rsc{rp}"])
                    if call == 1 and G == 0 and lhc == 0 and tt == 0:
                        dbg_tap("rh", rsc[rp][:], TN, [f"rsc{rp}"])
                    sch.add("vector", lambda v, rp=rp, lhc=lhc, tt=tt: v.tensor_tensor(
                        hps(lhc, tt), rsc[rp][:], rsc[rp][:], OP.mult),
                        reads=[f"rsc{rp}"], writes=[f"HP:{lhc}:{tt}"])
                    sch.add("vector", lambda v, lhc=lhc, tt=tt: v.tensor_tensor(
                        hps(lhc, tt), hps(lhc, tt), cb_sb[:, tt * TN:(tt + 1) * TN], OP.mult),
                        reads=[f"cb:{tt}", f"HP:{lhc}:{tt}"], writes=[f"HP:{lhc}:{tt}"])
            YB8 = [PH[0], PH[1], PH[2], PH[3], PY[0], PY[1], PC, PD]
            YT8 = ["PH0", "PH1", "PH2", "PH3", "PY0", "PY1", "PC", "PD"]
            for dc in range(NKC):
                slot = (G * NKC + dc) % 2
                sch.add("gpsimd", lambda g, dc=dc, slot=slot, G=G: g.dma_start(
                    out=w2r[slot][:].rearrange("p (h m) -> p h m", h=GS),
                    in_=w2h[dc, G]),
                    writes=[f"w2r{slot}"], dma_key=f"w2s{slot}")
                for ttg in range(2):
                    tts = [2 * ttg, 2 * ttg + 1]
                    for lhc in range(GS):
                        for tt in tts:
                            yb = YB8[(dc * 4 + tt) % 8]
                            sch.add("tensor", lambda t, slot=slot, lhc=lhc, tt=tt, yb=yb: t.matmul(
                                yb[:], w2r[slot][:, lhc * 128:(lhc + 1) * 128],
                                hps(lhc, tt), start=(lhc == 0), stop=(lhc == GS - 1)),
                                reads=[f"w2r{slot}", f"HP:{lhc}:{tt}"],
                                writes=[YT8[(dc * 4 + tt) % 8]])
                    for tt in tts:
                        yb = YB8[(dc * 4 + tt) % 8]
                        sch.add("vector", lambda v, dc=dc, tt=tt, yb=yb: v.tensor_tensor(
                            xs(dc, tt), xs(dc, tt), yb[:], OP.add),
                            reads=[f"X:{dc}:{tt}", YT8[(dc * 4 + tt) % 8]],
                            writes=[f"X:{dc}:{tt}"])

        def emit_ffn_call(call, with_ln_router=True):
            cv = f"c{call}"
            if with_ln_router:
                for tt in range(NTT):
                    emit_ln(call, tt,
                            lambda kc, tt=tt: (xns(kc, tt), f"XN:{kc}:{tt}:{cv}"),
                            cv,
                            dest3_fn=lambda tt=tt: XN[:].rearrange(
                                "p (k s) -> p k s", k=NKC)[:, :, tt * TN:(tt + 1) * TN])
            if call == 1:
                dbg_tap("xn", XN[:, 0:SEQ], SEQ,
                        [f"XN:0:{t}:{cv}" for t in range(NTT)])
                dbg_tap("xn", XN[:, SEQ:2 * SEQ], SEQ,
                        [f"XN:1:{t}:{cv}" for t in range(NTT)])
            if with_ln_router:
                emit_router(call)
            if call == 1:
                dbg_tap("ct", ct_sb[0:NE, :], SEQ, [f"ct:{t}" for t in range(4)])
            done_e = set()
            for G in range(NG):
                e = GROUP_EXPERT[G]
                if e not in done_e:
                    done_e.add(e)
                    emit_cb(call, e)
                    if call == 1 and e == 0:
                        dbg_tap("cb", cb_sb[:, :], SEQ, [f"cb:{t}" for t in range(NTT)])
                emit_ffn_group(call, G)
                if call == 1 and G == 0:
                    dbg_tap("hp", HP[:, 0:SEQ], SEQ,
                            [f"HP:0:{t}" for t in range(NTT)])
                    dbg_tap("hp", HP[:, SEQ:2 * SEQ], SEQ,
                            [f"HP:1:{t}" for t in range(NTT)])

        def xnts(par, kc):
            return xnt[par][:, kc * TN:(kc + 1) * TN]

        def emit_ln2(tt):
            par = tt % 2
            emit_ln(2, tt,
                    lambda kc, par=par: (xnts(par, kc), f"XT:{par}:{kc}"),
                    f"a{tt}",
                    dest3_fn=lambda par=par: xnt[par][:].rearrange(
                        "p (k s) -> p k s", k=NKC))

        for ti in range(NTI):
            p = ti % 3
            sch.add("sync", lambda sy, p=p, ti=ti: sy.dma_start(
                out=SIO[p][:], in_=xd[ti * 128:(ti + 1) * 128, :]),
                writes=[f"SIO{p}"], dma_key=f"sio{p}")
            for dc in range(NKC):
                pt = PH[2] if dc < 4 else PH[3]
                ptok = "PH2" if dc < 4 else "PH3"
                sch.add("tensor", lambda t, p=p, dc=dc, pt=pt: t.transpose(
                    pt[:, (dc % 4) * 128:(dc % 4 + 1) * 128],
                    SIO[p][:, dc * 128:(dc + 1) * 128], idn[:]),
                    reads=[f"SIO{p}", "idn"], writes=[ptok])
            for dc in range(NKC):
                pt = PH[2] if dc < 4 else PH[3]
                ptok = "PH2" if dc < 4 else "PH3"
                ev = "scalar" if dc < 4 else "vector"
                dst = X[:, dc * SEQ + ti * 128: dc * SEQ + (ti + 1) * 128]
                src = pt[:, (dc % 4) * 128:(dc % 4 + 1) * 128]
                if ev == "scalar":
                    sch.add("scalar", lambda s, dst=dst, src=src: s.copy(dst, src),
                            reads=[ptok], writes=[f"X:{dc}:{ti // 4}"])
                else:
                    sch.add("vector", lambda v, dst=dst, src=src: v.tensor_copy(dst, src),
                            reads=[ptok], writes=[f"X:{dc}:{ti // 4}"])
        for tt in range(NTT):
            emit_ln(1, tt,
                    lambda kc, tt=tt: (xns(kc, tt), f"XN:{kc}:{tt}:c1"),
                    "c1",
                    dest3_fn=lambda tt=tt: XN[:].rearrange(
                        "p (k s) -> p k s", k=NKC)[:, :, tt * TN:(tt + 1) * TN])
            emit_router_tt(1, tt)
        emit_router_fin(1)

        with contextlib.ExitStack() as ffn1_scope:
            fent = ffn1_scope.enter_context
            w1r = [fent(nc.sbuf_tensor(f"w1r{i}_a", [128, NKC * 128], BF16)) for i in range(4)]
            w2r = [fent(nc.sbuf_tensor(f"w2r{i}_a", [128, GS * 128], BF16)) for i in range(2)]
            rsc = [fent(nc.sbuf_tensor(f"rsc{i}_a", [128, TN], BF16)) for i in range(3)]
            emit_ffn_call(1)

        if stages >= 2:
            emit_ln2(0)
            emit_ln2(1)
            sch.barrier()
            emit_attn_sublayer = True
        else:
            emit_attn_sublayer = False
        with contextlib.ExitStack() as attn_scope:
            aent = attn_scope.enter_context
            Atp = [XN[:, 0:NKC * TN], XN[:, NKC * TN:2 * NKC * TN]]
            Btp = [HP[:, 0:NKC * TN], HP[:, NKC * TN:2 * NKC * TN]]
            Htp = [HP[:, 2 * NKC * TN:3 * NKC * TN], HP[:, 3 * NKC * TN:4 * NKC * TN]]
            pw = [[aent(nc.sbuf_tensor(f"pw{j}{i}", [128, NKC * 128], BF16))
                   for i in range(2)] for j in range(3)]
            srs = [aent(nc.sbuf_tensor("srs0", [128, TN], F32))]
            sis = [aent(nc.sbuf_tensor("sis0", [128, TN], F32))]
            sin_ = [aent(nc.sbuf_tensor("sinb0", [128, TN], F32))]
            WSRC = [wih, wrh, wxh]

            def emit_proj(tt, scan_tt=None, wout_tt=None):
                par = tt % 2
                for oc in range(NKC):
                    wpar = oc % 2
                    for j in range(3):
                        sch.add("gpsimd", lambda g, j=j, wpar=wpar, oc=oc: g.dma_start(
                            out=pw[j][wpar][:].rearrange("p (k m) -> p k m", k=NKC),
                            in_=WSRC[j][oc]),
                            writes=[f"pw{j}{wpar}"], dma_key=f"pw{j}{wpar}")
                    banks = [PH[0], PH[1], PH[2]] if oc % 2 == 0 else [PH[3], PY[0], PY[1]]
                    btok = ["PH0", "PH1", "PH2"] if oc % 2 == 0 else ["PH3", "PY0", "PY1"]
                    for j in range(3):
                        for kc in range(NKC):
                            sch.add("tensor", lambda t, j=j, kc=kc, wpar=wpar, par=par, banks=banks: t.matmul(
                                banks[j][:], pw[j][wpar][:, kc * 128:(kc + 1) * 128],
                                xnts(par, kc), start=(kc == 0), stop=(kc == NKC - 1)),
                                reads=[f"pw{j}{wpar}", f"XT:{par}:{kc}"],
                                writes=[btok[j]])
                    sp = 0
                    sch.add("scalar", lambda s, sp=sp, banks=banks, oc=oc: s.activation(
                        sin_[sp][:], banks[0][:], AF.Identity, bias=bia[:, 0 + oc:1 + oc]),
                        reads=[btok[0], "bia"], writes=[f"sin{sp}"])
                    sch.add("scalar", lambda s, sp=sp, banks=banks, oc=oc: s.activation(
                        srs[sp][:], banks[1][:], AF.Sigmoid, bias=bia[:, 8 + oc:9 + oc]),
                        reads=[btok[1], "bia"], writes=[f"srs{sp}"])
                    sch.add("scalar", lambda s, sp=sp, banks=banks, oc=oc: s.activation(
                        sis[sp][:], banks[2][:], AF.Sigmoid, bias=bia[:, 16 + oc:17 + oc]),
                        reads=[btok[2], "bia"], writes=[f"sis{sp}"])
                    asl = Atp[par][:, oc * TN:(oc + 1) * TN]
                    bsl = Btp[par][:, oc * TN:(oc + 1) * TN]
                    sch.add("vector", lambda v, sp=sp, asl=asl, oc=oc: v.tensor_scalar(
                        asl, srs[sp][:], a_sb[:, oc:oc + 1], None, OP.mult, OP.bypass),
                        reads=[f"srs{sp}", "a"], writes=[f"A:{par}:{oc}"])
                    sch.add("vector", lambda v, sp=sp, oc=oc: v.tensor_scalar(
                        srs[sp][:], srs[sp][:], na_sb[:, oc:oc + 1], 1.0, OP.mult, OP.add),
                        reads=[f"srs{sp}", "na"], writes=[f"srs{sp}"])
                    sch.add("vector", lambda v, sp=sp: v.tensor_tensor(
                        sis[sp][:], sis[sp][:], sin_[sp][:], OP.mult),
                        reads=[f"sis{sp}", f"sin{sp}"], writes=[f"sis{sp}"])
                    sch.add("vector", lambda v, sp=sp, bsl=bsl, oc=oc: v.tensor_tensor(
                        bsl, srs[sp][:], sis[sp][:], OP.mult),
                        reads=[f"srs{sp}", f"sis{sp}"], writes=[f"B:{par}:{oc}"])
                    if scan_tt is not None and oc < 4:
                        emit_scan(scan_tt, only_fc=2 * oc)
                        emit_scan(scan_tt, only_fc=2 * oc + 1)
                    if wout_tt is not None and oc >= 4:
                        emit_wout_chain(wout_tt, 2 * (oc - 4))
                        emit_wout_chain(wout_tt, 2 * (oc - 4) + 1)

            def emit_scan(tt, only_fc=None):
                par = tt % 2
                fcs = range(NKC) if only_fc is None else [only_fc]
                for fc in fcs:
                    hsl = Htp[par][:, fc * TN:(fc + 1) * TN]
                    asl = Atp[par][:, fc * TN:(fc + 1) * TN]
                    bsl = Btp[par][:, fc * TN:(fc + 1) * TN]
                    init = 0.0 if tt == 0 else carry[:, fc:fc + 1]
                    sch.add("vector", lambda v, hsl=hsl, asl=asl, bsl=bsl, init=init: v.tensor_tensor_scan(
                        hsl, asl, bsl, init, OP.mult, OP.add),
                        reads=[f"A:{par}:{fc}", f"B:{par}:{fc}"] + ([] if tt == 0 else [f"cr:{fc}"]),
                        writes=[f"H:{par}:{fc}"])
                    sch.add("vector", lambda v, fc=fc, par=par: v.tensor_copy(
                        carry[:, fc:fc + 1], Htp[par][:, fc * TN + TN - 1: fc * TN + TN]),
                        reads=[f"H:{par}:{fc}"], writes=[f"cr:{fc}"])

            def emit_wout_chain(tt, dc, ybanks=None):
                wpar = dc % 2
                sch.add("gpsimd", lambda g, dc=dc, wpar=wpar: g.dma_start(
                    out=wor[wpar][:].rearrange("p (k m) -> p k m", k=NKC),
                    in_=woh[dc]),
                    writes=[f"wor{wpar}"], dma_key=f"wo{wpar}")
                if ybanks is None:
                    pb = PC if dc % 2 == 0 else PD
                    pbt = "PC" if dc % 2 == 0 else "PD"
                else:
                    pb, pbt = ybanks[dc % len(ybanks)]
                par = tt % 2
                for ec in range(NKC):
                    sch.add("tensor", lambda t, wpar=wpar, ec=ec, pb=pb, par=par: t.matmul(
                        pb[:], wor[wpar][:, ec * 128:(ec + 1) * 128],
                        Htp[par][:, ec * TN:(ec + 1) * TN], start=(ec == 0), stop=(ec == NKC - 1)),
                        reads=[f"wor{wpar}", f"H:{par}:{ec}"], writes=[pbt])
                sch.add("vector", lambda v, dc=dc, tt=tt, pb=pb: v.scalar_tensor_tensor(
                    xs(dc, tt), pb[:], bia[:, 24 + dc:25 + dc], xs(dc, tt),
                    OP.add, OP.add),
                    reads=[pbt, "bia", f"X:{dc}:{tt}"], writes=[f"X:{dc}:{tt}"])

            def emit_wout(tt):
                for dc in range(NKC):
                    emit_wout_chain(tt, dc)

            if emit_attn_sublayer:
                emit_proj(0)
                for tt in range(NTT):
                    if tt + 2 < NTT:
                        emit_ln2(tt + 2)
                    if tt + 1 < NTT:
                        emit_proj(tt + 1, scan_tt=tt, wout_tt=tt)
                    elif stages == 2:
                        emit_scan(tt)
                        emit_wout(tt)

        if stages >= 3:
            sch.barrier()
            emit_scan(3)
            WB = [(PH[0], "PH0"), (PH[1], "PH1"), (PH[2], "PH2"), (PH[3], "PH3")]

            def _ln3(tt):
                emit_ln(3, tt,
                        lambda kc, tt=tt: (xns(kc, tt), f"XN:{kc}:{tt}:c3"),
                        "c3",
                        dest3_fn=lambda tt=tt: XN[:].rearrange(
                            "p (k s) -> p k s", k=NKC)[:, :, tt * TN:(tt + 1) * TN])
            _ln3(0)
            for dc in (0, 1, 2):
                emit_wout_chain(3, dc, ybanks=WB)
            _ln3(1)
            for dc in (3, 4, 5):
                emit_wout_chain(3, dc, ybanks=WB)
            _ln3(2)
            for dc in (6, 7):
                emit_wout_chain(3, dc, ybanks=WB)
            _ln3(3)
            emit_router(3)

        with contextlib.ExitStack() as ffn2_scope:
            fent = ffn2_scope.enter_context
            w1r = [fent(nc.sbuf_tensor(f"w1r{i}_b", [128, NKC * 128], BF16)) for i in range(4)]
            w2r = [fent(nc.sbuf_tensor(f"w2r{i}_b", [128, GS * 128], BF16)) for i in range(2)]
            rsc = [fent(nc.sbuf_tensor(f"rsc{i}_b", [128, TN], BF16)) for i in range(3)]
            if stages >= 3:
                emit_ffn_call(3, with_ln_router=False)

        for ti in range(NTI):
            p = ti % 3
            for dc in range(NKC):
                pt = PC if dc < 4 else PD
                ptok = "PC" if dc < 4 else "PD"
                src = X[:, dc * SEQ + ti * 128: dc * SEQ + (ti + 1) * 128]
                sch.add("tensor", lambda t, pt=pt, src=src, dc=dc: t.transpose(
                    pt[:, (dc % 4) * 128:(dc % 4 + 1) * 128], src, idn[:]),
                    reads=[f"X:{dc}:{ti // 4}", "idn"], writes=[ptok])
            for half, (pt, ptok) in enumerate([(PC, "PC"), (PD, "PD")]):
                dst = SIO[p][:, half * TN:(half + 1) * TN]
                sch.add("scalar", lambda s, dst=dst, pt=pt: s.copy(dst, pt[:]),
                        reads=[ptok],
                        writes=[f"SIO{p}"])
            sch.add("sync", lambda sy, p=p, ti=ti: sy.dma_start(
                out=outd[ti * 128:(ti + 1) * 128, :], in_=SIO[p][:]),
                reads=[f"SIO{p}"], writes=[f"OD:{ti}"], dma_key=f"sio{p}")
        sch.add("sync", lambda sy: sy.nop(),
                extra_deps=[sch.tok_w[f"OD:{ti}"]["sync"] for ti in range(NTI)],
                reads=[f"OD:{ti}" for ti in range(NTI)]
                + (["DBGOUT"] if debug else []))

        with nc.Block() as block:
            sch.emit(block, eng_sems, dma_sems)

    return nc


_NC_CACHE = None
_LAST_RES = None


def kernel(L0_attn_x, L0_attn_W_in, L0_attn_b_in, L0_attn_W_r, L0_attn_b_r,
           L0_attn_W_i, L0_attn_b_i, L0_attn_W_out, L0_attn_b_out, L0_attn_log_a,
           L0_ffn_router_w, L0_ffn_w1, L0_ffn_w2):
    x = np.ascontiguousarray(np.asarray(L0_attn_x, np.float32))

    def prep_sq(W):
        W = np.asarray(W, np.float32)
        return np.ascontiguousarray(W.reshape(8, 128, 8, 128).transpose(2, 1, 0, 3))

    w1hh = np.ascontiguousarray(
        np.asarray(L0_ffn_w1, np.float32).reshape(8, 128, 96, 128).transpose(2, 1, 0, 3))
    w2hh = np.ascontiguousarray(
        np.asarray(L0_ffn_w2, np.float32).reshape(12, 8, 128, 8, 128).transpose(3, 0, 2, 1, 4))
    wih = prep_sq(L0_attn_W_in)
    wrh = prep_sq(L0_attn_W_r)
    wxh = prep_sq(L0_attn_W_i)
    woh = prep_sq(L0_attn_W_out)
    rt = np.ascontiguousarray(
        np.asarray(L0_ffn_router_w, np.float32).T.reshape(8, 128, 8).transpose(1, 0, 2))

    def col(v):
        return np.asarray(v, np.float32).reshape(8, 128).T

    bcol = np.ascontiguousarray(np.concatenate(
        [col(L0_attn_b_in), col(L0_attn_b_r), col(L0_attn_b_i),
         col(L0_attn_b_out), col(L0_attn_log_a)], axis=1))
    sel = np.zeros((8, 1024), np.float32)
    for e in range(8):
        sel[e, e * 128:(e + 1) * 128] = 1.0
    onescol = np.ones((128, 1), np.float32)
    onesrow = np.ones((1, 128), np.float32)
    ident = np.eye(128, dtype=np.float32)
    epsv = np.full((128, 1), 1e-5, np.float32)

    stages = int(os.environ.get("MK_STAGES", "3"))
    dbgtag = os.environ.get("MK_DEBUG") or None
    global _NC_CACHE
    key = (stages, dbgtag)
    if _NC_CACHE is None or _NC_CACHE[0] != key:
        _NC_CACHE = (key, build_nc(stages, dbgtag))
    nc = _NC_CACHE[1]

    shared = {"w1h": w1hh, "w2h": w2hh, "wih": wih, "wrh": wrh, "wxh": wxh,
              "woh": woh, "rt": rt, "bcol": bcol, "sel": sel,
              "onescol": onescol, "onesrow": onesrow, "ident": ident, "epsd": epsv}
    in_maps = [dict(shared, x=np.ascontiguousarray(x[c])) for c in range(8)]
    res = run_bass_kernel_spmd(nc, in_maps, core_ids=list(range(8)))
    global _LAST_RES
    _LAST_RES = res
    out = np.stack([np.asarray(r["out"]) for r in res.results], axis=0)
    return out


# version 33
# speedup vs baseline: 1.0052x; 1.0018x over previous
import contextlib
import os
import numpy as np
import concourse.bass as bass
import concourse.mybir as mybir
from concourse.bass_utils import run_bass_kernel_spmd
from concourse.alu_op_type import AluOpType as OP

F32 = mybir.dt.float32
BF16 = mybir.dt.bfloat16
AF = mybir.ActivationFunctionType
AX = mybir.AxisListType

D = 1024
SEQ = 2048
NKC = 8
NTT = 4
TN = 512
NTI = 16
NHC = 96
NG = 12
GS = 8
NE = 8
GROUP_EXPERT = [0, 1, 2, 3, 4, 4, 5, 5, 6, 6, 7, 7]

ENGS = ("tensor", "vector", "scalar", "gpsimd", "sync")
SELFSYNC = {"vector", "scalar", "gpsimd"}


class Sched:

    def __init__(self):
        self.ops = []
        self.tok_w = {}
        self.tok_r = {}
        self.last_op = {}
        self.dma_count = {}

    def add(self, eng, fn, reads=(), writes=(), dma_key=None, extra_deps=()):
        deps = {}
        for t in reads:
            for i in self.tok_w.get(t, {}).values():
                deps[i] = True
        for t in writes:
            for i in self.tok_w.get(t, {}).values():
                deps[i] = True
            for i in self.tok_r.get(t, {}).values():
                deps[i] = True
        for d in extra_deps:
            if d is not None:
                deps[d] = True
        opid = len(self.ops)
        dma = None
        if dma_key is not None:
            n = self.dma_count.get(dma_key, 0) + 1
            self.dma_count[dma_key] = n
            dma = (dma_key, 16 * n)
        deps.pop(opid, None)
        self.ops.append(dict(eng=eng, fn=fn, deps=sorted(deps), dma=dma,
                             inc=False, cum=0))
        for t in reads:
            self.tok_r.setdefault(t, {})[eng] = opid
        for t in writes:
            self.tok_w[t] = {eng: opid}
            self.tok_r[t] = {}
        self.last_op[eng] = opid
        return opid

    def barrier(self):
        lasts = dict(self.last_op)
        for e in ENGS:
            dep = [v for k, v in lasts.items() if k != e]
            self.add(e, lambda eng: eng.nop(), extra_deps=dep)

    def analyze(self):
        for o in self.ops:
            for d in o["deps"]:
                p = self.ops[d]
                if p["dma"] is not None:
                    continue
                if p["eng"] == o["eng"] and p["eng"] not in SELFSYNC:
                    continue
                p["inc"] = True
        cum = {e: 0 for e in ENGS}
        for o in self.ops:
            if o["inc"]:
                cum[o["eng"]] += 1
            o["cum"] = cum[o["eng"]]

    def emit(self, block, eng_sems, dma_sems):
        self.analyze()
        streams = {e: [] for e in ENGS}
        for o in self.ops:
            streams[o["eng"]].append(o)

        def mk(ename):
            ops = streams[ename]

            def body(eng):
                lastw = {}
                for o in ops:
                    wants = {}
                    for d in o["deps"]:
                        p = self.ops[d]
                        if p["dma"] is not None:
                            k = ("dma", p["dma"][0])
                            v = p["dma"][1]
                        else:
                            if p["eng"] == ename and ename not in SELFSYNC:
                                continue
                            k = ("eng", p["eng"])
                            v = p["cum"]
                        if wants.get(k, 0) < v:
                            wants[k] = v
                    for k in sorted(wants):
                        v = wants[k]
                        if lastw.get(k, 0) >= v:
                            continue
                        sem = dma_sems[k[1]] if k[0] == "dma" else eng_sems[k[1]]
                        eng.wait_ge(sem, v)
                        lastw[k] = v
                    inst = o["fn"](eng)
                    if o["dma"] is not None:
                        inst.then_inc(dma_sems[o["dma"][0]], 16)
                    elif o["inc"]:
                        inst.then_inc(eng_sems[ename], 1)
            return body

        for e in ENGS:
            getattr(block, e)(mk(e))


def build_nc(stages=3, debug=None):
    nc = bass.Bass("TRN2", target_bir_lowering=False)

    xd = nc.dram_tensor("x", [SEQ, D], F32, kind="ExternalInput").ap()
    outd = nc.dram_tensor("out", [SEQ, D], F32, kind="ExternalOutput").ap()
    w1h = nc.dram_tensor("w1h", [NHC, 128, NKC, 128], F32, kind="ExternalInput").ap()
    w2h = nc.dram_tensor("w2h", [NKC, NG, 128, GS, 128], F32, kind="ExternalInput").ap()
    wih = nc.dram_tensor("wih", [NKC, 128, NKC, 128], F32, kind="ExternalInput").ap()
    wrh = nc.dram_tensor("wrh", [NKC, 128, NKC, 128], F32, kind="ExternalInput").ap()
    wxh = nc.dram_tensor("wxh", [NKC, 128, NKC, 128], F32, kind="ExternalInput").ap()
    woh = nc.dram_tensor("woh", [NKC, 128, NKC, 128], F32, kind="ExternalInput").ap()
    rtd = nc.dram_tensor("rt", [128, NKC, NE], F32, kind="ExternalInput").ap()
    bcol = nc.dram_tensor("bcol", [128, 40], F32, kind="ExternalInput").ap()
    seld = nc.dram_tensor("sel", [NE, NE * 128], F32, kind="ExternalInput").ap()
    onescol = nc.dram_tensor("onescol", [128, 1], F32, kind="ExternalInput").ap()
    onesrow = nc.dram_tensor("onesrow", [1, 128], F32, kind="ExternalInput").ap()
    identd = nc.dram_tensor("ident", [128, 128], F32, kind="ExternalInput").ap()
    epsd = nc.dram_tensor("epsd", [128, 1], F32, kind="ExternalInput").ap()
    dbgd = nc.dram_tensor("dbg", [128, 8192], F32, kind="ExternalOutput").ap()

    sch = Sched()

    with contextlib.ExitStack() as top:
        ent = top.enter_context
        X = ent(nc.sbuf_tensor("X", [128, NKC * SEQ], F32))
        XN = ent(nc.sbuf_tensor("XN", [128, NKC * SEQ], BF16))
        HP = ent(nc.sbuf_tensor("HP", [128, GS * SEQ], BF16))
        SIO = [ent(nc.sbuf_tensor(f"SIO{i}", [128, D], F32)) for i in range(3)]
        stB = ent(nc.sbuf_tensor("stB", [1, TN], F32))
        stC = ent(nc.sbuf_tensor("stC", [1, TN], F32))
        rb0 = ent(nc.sbuf_tensor("rb0", [128, TN], BF16))
        rb1 = ent(nc.sbuf_tensor("rb1", [128, TN], BF16))
        RB = [rb0, rb1]
        xsq = [ent(nc.sbuf_tensor(f"xsq{i}", [128, TN], BF16)) for i in range(2)]
        xbr = [ent(nc.sbuf_tensor(f"xbr{i}", [128, TN], BF16)) for i in range(2)]
        psb = ent(nc.sbuf_tensor("psb", [NE, TN], F32))
        ptw = ent(nc.sbuf_tensor("ptw", [128, 128], F32))
        cwb = ent(nc.sbuf_tensor("cwb", [128, 128], F32))
        m1s = ent(nc.sbuf_tensor("m1s", [128, 16], F32))
        m2s = ent(nc.sbuf_tensor("m2s", [128, 16], F32))
        bia = ent(nc.sbuf_tensor("bia", [128, 40], F32))
        a_sb = ent(nc.sbuf_tensor("a_sb", [128, NE], F32))
        na_sb = ent(nc.sbuf_tensor("na_sb", [128, NE], F32))
        rt_sb = ent(nc.sbuf_tensor("rt_sb", [128, NKC * NE], BF16))
        sel_sb = ent(nc.sbuf_tensor("sel_sb", [NE, NE * 128], BF16))
        cb_sb = ent(nc.sbuf_tensor("cb_sb", [128, SEQ], BF16))
        xnt = [ent(nc.sbuf_tensor(f"xnt{i}", [128, NKC * TN], BF16)) for i in range(2)]
        ct_sb = ent(nc.sbuf_tensor("ct_sb", [NE, SEQ], BF16))
        ocol = ent(nc.sbuf_tensor("ocol", [128, 1], F32))
        ocolb = ent(nc.sbuf_tensor("ocolb", [128, 1], BF16))
        orow = ent(nc.sbuf_tensor("orow", [1, 128], F32))
        orowb = ent(nc.sbuf_tensor("orowb", [1, 128], BF16))
        stMb = ent(nc.sbuf_tensor("stMb", [1, TN], BF16))
        stRb = ent(nc.sbuf_tensor("stRb", [1, TN], BF16))
        idn = ent(nc.sbuf_tensor("idn", [128, 128], F32))
        carry = ent(nc.sbuf_tensor("carry", [128, NKC], F32))
        eps_sb = ent(nc.sbuf_tensor("eps_sb", [128, 1], F32))
        wor = [ent(nc.sbuf_tensor(f"wor{i}", [128, NKC * 128], BF16)) for i in range(2)]

        PH = [ent(nc.psum_tensor(f"PH{i}", [128, TN], F32)) for i in range(4)]
        PY = [ent(nc.psum_tensor(f"PY{i}", [128, TN], F32)) for i in range(2)]
        PC = ent(nc.psum_tensor("PC", [128, TN], F32))
        PD = ent(nc.psum_tensor("PD", [128, TN], F32))

        eng_sems = {e: ent(nc.semaphore(f"se_{e}")) for e in ENGS}
        dkeys = (["cbia", "crt", "csel", "cocol", "cocolb", "corow", "cidn",
                  "ceps"] + [f"cdbg{i}" for i in range(8)]
                 + [f"w1s{i}" for i in range(4)]
                 + [f"w2s{i}" for i in range(2)] + [f"sio{i}" for i in range(3)] + ["corowb"]
                 + [f"pw{j}{p}" for j in range(3) for p in range(2)]
                 + [f"wo{p}" for p in range(2)])
        dma_sems = {k: ent(nc.semaphore(f"sd_{k}")) for k in dkeys}

        dbg_state = {"col": 0}

        def dbg_tap(tag, ap, w, tokens):
            if debug != tag:
                return
            c0 = dbg_state["col"]
            dbg_state["col"] += w
            np_ = ap.shape[0]
            sch.add("gpsimd", lambda g, ap=ap, c0=c0, w=w, np_=np_: g.dma_start(
                out=dbgd[0:np_, c0:c0 + w], in_=ap), reads=list(tokens),
                writes=["DBGOUT"], dma_key="cdbg0")

        def xs(dc, tt, w=TN):
            return X[:, dc * SEQ + tt * w: dc * SEQ + (tt + 1) * w]

        def xns(kc, tt, w=TN):
            return XN[:, kc * SEQ + tt * w: kc * SEQ + (tt + 1) * w]

        def hps(lhc, tt):
            return HP[:, lhc * SEQ + tt * TN: lhc * SEQ + (tt + 1) * TN]

        sch.add("gpsimd", lambda g: g.dma_start(out=bia[:], in_=bcol),
                writes=["bia"], dma_key="cbia")
        sch.add("gpsimd", lambda g: g.dma_start(
            out=rt_sb[:].rearrange("p (k e) -> p k e", k=NKC), in_=rtd),
            writes=["rt"], dma_key="crt")
        sch.add("gpsimd", lambda g: g.dma_start(out=sel_sb[:], in_=seld),
                writes=["sel"], dma_key="csel")
        sch.add("gpsimd", lambda g: g.dma_start(out=ocol[:], in_=onescol),
                writes=["ocol"], dma_key="cocol")
        sch.add("gpsimd", lambda g: g.dma_start(out=ocolb[:], in_=onescol),
                writes=["ocolb"], dma_key="cocolb")
        sch.add("gpsimd", lambda g: g.dma_start(out=orow[:], in_=onesrow),
                writes=["orow"], dma_key="corow")
        sch.add("gpsimd", lambda g: g.dma_start(out=orowb[:], in_=onesrow),
                writes=["orowb"], dma_key="corowb")
        sch.add("gpsimd", lambda g: g.dma_start(out=idn[:], in_=identd),
                writes=["idn"], dma_key="cidn")
        sch.add("gpsimd", lambda g: g.dma_start(out=eps_sb[:], in_=epsd),
                writes=["eps"], dma_key="ceps")
        sch.add("scalar", lambda s: s.activation(a_sb[:], bia[:, 32:40], AF.Sigmoid),
                reads=["bia"], writes=["a"])
        sch.add("vector", lambda v: v.tensor_scalar(
            na_sb[:], a_sb[:], -1.0, None, OP.mult, OP.bypass),
            reads=["a"], writes=["na"])


        def emit_ln(call, tt, dest_fn, dest_tok, dest3_fn=None):
            cv = f"c{call}"
            for kc in range(NKC):
                xsl = xs(kc, tt)
                xb = xbr[kc % 2]
                xq = xsq[kc % 2]
                sch.add("scalar", lambda s, xb=xb, xsl=xsl: s.copy(xb[:], xsl),
                        reads=[f"X:{kc}:{tt}"], writes=[f"xbr{kc % 2}"])
                sch.add("vector", lambda v, xb=xb, xq=xq: v.tensor_tensor(
                    xq[:], xb[:], xb[:], OP.mult),
                    reads=[f"xbr{kc % 2}"], writes=[f"xsq{kc % 2}"])
                sch.add("tensor", lambda t, kc=kc, xb=xb: t.matmul(
                    PC[0:1, :], ocolb[:], xb[:], start=(kc == 0), stop=(kc == NKC - 1)),
                    reads=[f"xbr{kc % 2}", "ocolb"], writes=["PC"])
                sch.add("tensor", lambda t, kc=kc, xq=xq: t.matmul(
                    PD[0:1, :], ocolb[:], xq[:], start=(kc == 0), stop=(kc == NKC - 1)),
                    reads=[f"xsq{kc % 2}", "ocolb"], writes=["PD"])
            sch.add("vector", lambda v: v.tensor_scalar(
                stMb[:], PC[0:1, :], 1.0 / D, None, OP.mult, OP.bypass),
                reads=["PC"], writes=["stMb"])
            sch.add("vector", lambda v: v.tensor_scalar(
                stB[:], PD[0:1, :], 1.0 / D, None, OP.mult, OP.bypass),
                reads=["PD"], writes=["stB"])
            sch.add("vector", lambda v: v.tensor_tensor(stC[:], stMb[:], stMb[:], OP.mult),
                    reads=["stMb"], writes=["stC"])
            sch.add("vector", lambda v: v.tensor_tensor(stRb[:], stB[:], stC[:], OP.subtract),
                    reads=["stB", "stC"], writes=["stRb"])
            sch.add("tensor", lambda t: t.matmul(PC[:], orowb[:], stMb[:], start=True, stop=True),
                    reads=["stMb", "orowb", "PC"], writes=["PC"])
            sch.add("tensor", lambda t: t.matmul(PD[:], orowb[:], stRb[:], start=True, stop=True),
                    reads=["stRb", "orowb", "PD"], writes=["PD"])
            sch.add("scalar", lambda s: s.activation(PD[:], PD[:], AF.Ln,
                                                     bias=eps_sb[:, 0:1]),
                    reads=["PD", "eps"], writes=["PD"])
            rbp = RB[tt % 2]
            sch.add("scalar", lambda s, rbp=rbp: s.activation(rbp[:], PD[:], AF.Exp, scale=-0.5),
                    reads=["PD"], writes=[f"rb{tt % 2}"])
            dst3 = dest3_fn()
            x3 = X[:].rearrange("p (k s) -> p k s", k=NKC)[:, :, tt * TN:(tt + 1) * TN]
            mb3 = PC[:].rearrange("p (k s) -> p k s", k=1)
            rb3 = rbp[:].rearrange("p (k s) -> p k s", k=1)
            dtoks = [dest_fn(kc)[1] for kc in range(NKC)]
            xtoks = [f"X:{kc}:{tt}" for kc in range(NKC)]
            b_x, b_m = bass.broadcast_tensor_aps(x3, mb3)
            sch.add("vector", lambda v, dst3=dst3, b_x=b_x, b_m=b_m: v.tensor_tensor(
                dst3, b_x, b_m, OP.subtract),
                reads=xtoks + ["PC"], writes=dtoks)
            d3b, b_r = bass.broadcast_tensor_aps(dst3, rb3)
            sch.add("vector", lambda v, dst3=dst3, d3b=d3b, b_r=b_r: v.tensor_tensor(
                dst3, d3b, b_r, OP.mult),
                reads=dtoks + [f"rb{tt % 2}"], writes=dtoks)

        def emit_router_tt(call, tt):
            cv = f"c{call}"
            for kc in range(NKC):
                sch.add("tensor", lambda t, kc=kc, tt=tt: t.matmul(
                    PY[0][0:NE, :], rt_sb[:, kc * NE:(kc + 1) * NE], xns(kc, tt),
                    start=(kc == 0), stop=(kc == NKC - 1)),
                    reads=[f"XN:{kc}:{tt}:{cv}", "rt"], writes=["PY0"])
            sch.add("scalar", lambda s: s.activation(psb[:], PY[0][0:NE, :], AF.Sigmoid),
                    reads=["PY0"], writes=["psb"])
            for c4 in range(4):
                c = tt * 4 + c4
                sch.add("tensor", lambda t, c4=c4, c=c: t.transpose(
                    PH[1][:, c * 8:(c + 1) * 8], psb[:, c4 * 128:(c4 + 1) * 128], idn[0:NE, 0:NE]),
                    reads=["psb", "idn"], writes=["PH1"])

        def emit_router_fin(call):
            cv = f"c{call}"
            sch.add("vector", lambda v: v.tensor_copy(ptw[:], PH[1][:, 0:128]),
                    reads=["PH1"], writes=["ptw"])
            pt3 = ptw[:].rearrange("p (c e) -> p c e", e=8)
            cw3 = cwb[:].rearrange("p (c e) -> p c e", e=8)
            m13 = m1s[:].rearrange("p (c e) -> p c e", e=1)
            m23 = m2s[:].rearrange("p (c e) -> p c e", e=1)
            sch.add("vector", lambda v: v.tensor_reduce(m1s[:], pt3, AX.X, OP.max),
                    reads=["ptw"], writes=["m1"])
            b_p, b_m1 = bass.broadcast_tensor_aps(pt3, m13)
            sch.add("vector", lambda v: v.tensor_tensor(cw3, b_p, b_m1, OP.is_lt),
                    reads=["ptw", "m1"], writes=["cw"])
            sch.add("vector", lambda v: v.tensor_tensor(cw3, pt3, cw3, OP.mult),
                    reads=["ptw", "cw"], writes=["cw"])
            sch.add("vector", lambda v: v.tensor_reduce(m2s[:], cw3, AX.X, OP.max),
                    reads=["cw"], writes=["m2"])
            b_p2, b_m2 = bass.broadcast_tensor_aps(pt3, m23)
            sch.add("vector", lambda v: v.tensor_tensor(cw3, b_p2, b_m2, OP.is_ge),
                    reads=["ptw", "m2"], writes=["cw"])
            sch.add("vector", lambda v: v.tensor_tensor(m1s[:], m1s[:], m2s[:], OP.add),
                    reads=["m1", "m2"], writes=["m1"])
            sch.add("vector", lambda v: v.tensor_scalar(
                m1s[:], m1s[:], 2.0, 2e-20, OP.mult, OP.add),
                reads=["m1"], writes=["m1"])
            sch.add("scalar", lambda s: s.activation(m2s[:], m1s[:], AF.Ln),
                    reads=["m1"], writes=["m2"])
            sch.add("scalar", lambda s: s.activation(m2s[:], m2s[:], AF.Exp, scale=-1.0),
                    reads=["m2"], writes=["m2"])
            sch.add("vector", lambda v: v.tensor_tensor(cw3, cw3, pt3, OP.mult),
                    reads=["cw", "ptw"], writes=["cw"])
            b_c, b_r = bass.broadcast_tensor_aps(cw3, m23)
            sch.add("vector", lambda v: v.tensor_tensor(cw3, b_c, b_r, OP.mult),
                    reads=["cw", "m2"], writes=["cw"])
            for grp in range(4):
                for c4 in range(4):
                    c = grp * 4 + c4
                    sch.add("tensor", lambda t, c4=c4, c=c: t.transpose(
                        PC[0:NE, c4 * 128:(c4 + 1) * 128], cwb[:, c * 8:(c + 1) * 8], idn[:]),
                        reads=["cw", "idn"], writes=["PC"])
                sch.add("scalar", lambda s, grp=grp: s.copy(
                    ct_sb[:, grp * TN:(grp + 1) * TN], PC[0:NE, :]),
                    reads=["PC"], writes=[f"ct:{grp}"])

        def emit_router(call):
            for tt in range(NTT):
                emit_router_tt(call, tt)
            emit_router_fin(call)

        def emit_cb(call, e):
            cv = f"c{call}"
            for tt in range(NTT):
                sch.add("tensor", lambda t, e=e, tt=tt: t.matmul(
                    PH[0][:], sel_sb[:, e * 128:(e + 1) * 128],
                    ct_sb[0:NE, tt * TN:(tt + 1) * TN], start=True, stop=True),
                    reads=[f"ct:{tt}", "sel"], writes=["PH0"])
                sch.add("scalar", lambda s, tt=tt: s.copy(
                    cb_sb[:, tt * TN:(tt + 1) * TN], PH[0][:]),
                    reads=["PH0"], writes=[f"cb:{tt}"])

        def emit_ffn_group(call, G):
            cv = f"c{call}"
            e = GROUP_EXPERT[G]
            for lhc in range(GS):
                hc = G * GS + lhc
                slot = hc % 4
                sch.add("gpsimd", lambda g, hc=hc, slot=slot: g.dma_start(
                    out=w1r[slot][:].rearrange("p (k m) -> p k m", k=NKC),
                    in_=w1h[hc]),
                    writes=[f"w1r{slot}"], dma_key=f"w1s{slot}")
                BK6 = [PH[0], PH[1], PH[2], PH[3], PY[0], PY[1]]
                BT6 = ["PH0", "PH1", "PH2", "PH3", "PY0", "PY1"]
                bidx = lambda tt: (hc * NTT + tt) % 6
                for kc in range(NKC):
                    for tt in range(NTT):
                        hb = BK6[bidx(tt)]
                        sch.add("tensor", lambda t, slot=slot, kc=kc, tt=tt, hb=hb: t.matmul(
                            hb[:], w1r[slot][:, kc * 128:(kc + 1) * 128],
                            xns(kc, tt), start=(kc == 0), stop=(kc == NKC - 1)),
                            reads=[f"w1r{slot}", f"XN:{kc}:{tt}:{cv}"],
                            writes=[BT6[bidx(tt)]])
                for tt in range(NTT):
                    rp = tt
                    hb = BK6[bidx(tt)]
                    sch.add("scalar", lambda s, rp=rp, hb=hb: s.activation(
                        rsc[rp][:], hb[:], AF.Relu),
                        reads=[BT6[bidx(tt)]], writes=[f"rsc{rp}"])
                    if call == 1 and G == 0 and lhc == 0 and tt == 0:
                        dbg_tap("rh", rsc[rp][:], TN, [f"rsc{rp}"])
                    sch.add("vector", lambda v, rp=rp, lhc=lhc, tt=tt: v.tensor_tensor(
                        hps(lhc, tt), rsc[rp][:], rsc[rp][:], OP.mult),
                        reads=[f"rsc{rp}"], writes=[f"HP:{lhc}:{tt}"])
                    sch.add("vector", lambda v, lhc=lhc, tt=tt: v.tensor_tensor(
                        hps(lhc, tt), hps(lhc, tt), cb_sb[:, tt * TN:(tt + 1) * TN], OP.mult),
                        reads=[f"cb:{tt}", f"HP:{lhc}:{tt}"], writes=[f"HP:{lhc}:{tt}"])
            YB8 = [PH[0], PH[1], PH[2], PH[3], PY[0], PY[1], PC, PD]
            YT8 = ["PH0", "PH1", "PH2", "PH3", "PY0", "PY1", "PC", "PD"]
            for dc in range(NKC):
                slot = (G * NKC + dc) % 2
                sch.add("gpsimd", lambda g, dc=dc, slot=slot, G=G: g.dma_start(
                    out=w2r[slot][:].rearrange("p (h m) -> p h m", h=GS),
                    in_=w2h[dc, G]),
                    writes=[f"w2r{slot}"], dma_key=f"w2s{slot}")
                for ttg in range(2):
                    tts = [2 * ttg, 2 * ttg + 1]
                    for lhc in range(GS):
                        for tt in tts:
                            yb = YB8[(dc * 4 + tt) % 8]
                            sch.add("tensor", lambda t, slot=slot, lhc=lhc, tt=tt, yb=yb: t.matmul(
                                yb[:], w2r[slot][:, lhc * 128:(lhc + 1) * 128],
                                hps(lhc, tt), start=(lhc == 0), stop=(lhc == GS - 1)),
                                reads=[f"w2r{slot}", f"HP:{lhc}:{tt}"],
                                writes=[YT8[(dc * 4 + tt) % 8]])
                    for tt in tts:
                        yb = YB8[(dc * 4 + tt) % 8]
                        sch.add("vector", lambda v, dc=dc, tt=tt, yb=yb: v.tensor_tensor(
                            xs(dc, tt), xs(dc, tt), yb[:], OP.add),
                            reads=[f"X:{dc}:{tt}", YT8[(dc * 4 + tt) % 8]],
                            writes=[f"X:{dc}:{tt}"])

        def emit_ffn_call(call, with_ln_router=True):
            cv = f"c{call}"
            if with_ln_router:
                for tt in range(NTT):
                    emit_ln(call, tt,
                            lambda kc, tt=tt: (xns(kc, tt), f"XN:{kc}:{tt}:{cv}"),
                            cv,
                            dest3_fn=lambda tt=tt: XN[:].rearrange(
                                "p (k s) -> p k s", k=NKC)[:, :, tt * TN:(tt + 1) * TN])
            if call == 1:
                dbg_tap("xn", XN[:, 0:SEQ], SEQ,
                        [f"XN:0:{t}:{cv}" for t in range(NTT)])
                dbg_tap("xn", XN[:, SEQ:2 * SEQ], SEQ,
                        [f"XN:1:{t}:{cv}" for t in range(NTT)])
            if with_ln_router:
                emit_router(call)
            if call == 1:
                dbg_tap("ct", ct_sb[0:NE, :], SEQ, [f"ct:{t}" for t in range(4)])
            done_e = set()
            for G in range(NG):
                e = GROUP_EXPERT[G]
                if e not in done_e:
                    done_e.add(e)
                    emit_cb(call, e)
                    if call == 1 and e == 0:
                        dbg_tap("cb", cb_sb[:, :], SEQ, [f"cb:{t}" for t in range(NTT)])
                emit_ffn_group(call, G)
                if call == 1 and G == 0:
                    dbg_tap("hp", HP[:, 0:SEQ], SEQ,
                            [f"HP:0:{t}" for t in range(NTT)])
                    dbg_tap("hp", HP[:, SEQ:2 * SEQ], SEQ,
                            [f"HP:1:{t}" for t in range(NTT)])

        def xnts(par, kc):
            return xnt[par][:, kc * TN:(kc + 1) * TN]

        def emit_ln2(tt):
            par = tt % 2
            emit_ln(2, tt,
                    lambda kc, par=par: (xnts(par, kc), f"XT:{par}:{kc}"),
                    f"a{tt}",
                    dest3_fn=lambda par=par: xnt[par][:].rearrange(
                        "p (k s) -> p k s", k=NKC))

        for ti in range(NTI):
            p = ti % 3
            sch.add("sync", lambda sy, p=p, ti=ti: sy.dma_start(
                out=SIO[p][:], in_=xd[ti * 128:(ti + 1) * 128, :]),
                writes=[f"SIO{p}"], dma_key=f"sio{p}")
            for dc in range(NKC):
                pt = PH[2] if dc < 4 else PH[3]
                ptok = "PH2" if dc < 4 else "PH3"
                sch.add("tensor", lambda t, p=p, dc=dc, pt=pt: t.transpose(
                    pt[:, (dc % 4) * 128:(dc % 4 + 1) * 128],
                    SIO[p][:, dc * 128:(dc + 1) * 128], idn[:]),
                    reads=[f"SIO{p}", "idn"], writes=[ptok])
            for dc in range(NKC):
                pt = PH[2] if dc < 4 else PH[3]
                ptok = "PH2" if dc < 4 else "PH3"
                ev = "scalar" if dc < 4 else "vector"
                dst = X[:, dc * SEQ + ti * 128: dc * SEQ + (ti + 1) * 128]
                src = pt[:, (dc % 4) * 128:(dc % 4 + 1) * 128]
                if ev == "scalar":
                    sch.add("scalar", lambda s, dst=dst, src=src: s.copy(dst, src),
                            reads=[ptok], writes=[f"X:{dc}:{ti // 4}"])
                else:
                    sch.add("vector", lambda v, dst=dst, src=src: v.tensor_copy(dst, src),
                            reads=[ptok], writes=[f"X:{dc}:{ti // 4}"])
        for tt in range(NTT):
            emit_ln(1, tt,
                    lambda kc, tt=tt: (xns(kc, tt), f"XN:{kc}:{tt}:c1"),
                    "c1",
                    dest3_fn=lambda tt=tt: XN[:].rearrange(
                        "p (k s) -> p k s", k=NKC)[:, :, tt * TN:(tt + 1) * TN])
            emit_router_tt(1, tt)
        emit_router_fin(1)

        with contextlib.ExitStack() as ffn1_scope:
            fent = ffn1_scope.enter_context
            w1r = [fent(nc.sbuf_tensor(f"w1r{i}_a", [128, NKC * 128], BF16)) for i in range(4)]
            w2r = [fent(nc.sbuf_tensor(f"w2r{i}_a", [128, GS * 128], BF16)) for i in range(2)]
            rsc = [fent(nc.sbuf_tensor(f"rsc{i}_a", [128, TN], BF16)) for i in range(4)]
            emit_ffn_call(1)

        if stages >= 2:
            emit_ln2(0)
            emit_ln2(1)
            sch.barrier()
            emit_attn_sublayer = True
        else:
            emit_attn_sublayer = False
        with contextlib.ExitStack() as attn_scope:
            aent = attn_scope.enter_context
            Atp = [XN[:, 0:NKC * TN], XN[:, NKC * TN:2 * NKC * TN]]
            Btp = [HP[:, 0:NKC * TN], HP[:, NKC * TN:2 * NKC * TN]]
            Htp = [HP[:, 2 * NKC * TN:3 * NKC * TN], HP[:, 3 * NKC * TN:4 * NKC * TN]]
            pw = [[aent(nc.sbuf_tensor(f"pw{j}{i}", [128, NKC * 128], BF16))
                   for i in range(2)] for j in range(3)]
            srs = [aent(nc.sbuf_tensor("srs0", [128, TN], F32))]
            sis = [aent(nc.sbuf_tensor("sis0", [128, TN], F32))]
            sin_ = [aent(nc.sbuf_tensor("sinb0", [128, TN], F32))]
            WSRC = [wih, wrh, wxh]

            def emit_proj(tt, scan_tt=None, wout_tt=None):
                par = tt % 2
                for oc in range(NKC):
                    wpar = oc % 2
                    for j in range(3):
                        sch.add("gpsimd", lambda g, j=j, wpar=wpar, oc=oc: g.dma_start(
                            out=pw[j][wpar][:].rearrange("p (k m) -> p k m", k=NKC),
                            in_=WSRC[j][oc]),
                            writes=[f"pw{j}{wpar}"], dma_key=f"pw{j}{wpar}")
                    banks = [PH[0], PH[1], PH[2]] if oc % 2 == 0 else [PH[3], PY[0], PY[1]]
                    btok = ["PH0", "PH1", "PH2"] if oc % 2 == 0 else ["PH3", "PY0", "PY1"]
                    for j in range(3):
                        for kc in range(NKC):
                            sch.add("tensor", lambda t, j=j, kc=kc, wpar=wpar, par=par, banks=banks: t.matmul(
                                banks[j][:], pw[j][wpar][:, kc * 128:(kc + 1) * 128],
                                xnts(par, kc), start=(kc == 0), stop=(kc == NKC - 1)),
                                reads=[f"pw{j}{wpar}", f"XT:{par}:{kc}"],
                                writes=[btok[j]])
                    sp = 0
                    sch.add("scalar", lambda s, sp=sp, banks=banks, oc=oc: s.activation(
                        sin_[sp][:], banks[0][:], AF.Identity, bias=bia[:, 0 + oc:1 + oc]),
                        reads=[btok[0], "bia"], writes=[f"sin{sp}"])
                    sch.add("scalar", lambda s, sp=sp, banks=banks, oc=oc: s.activation(
                        srs[sp][:], banks[1][:], AF.Sigmoid, bias=bia[:, 8 + oc:9 + oc]),
                        reads=[btok[1], "bia"], writes=[f"srs{sp}"])
                    sch.add("scalar", lambda s, sp=sp, banks=banks, oc=oc: s.activation(
                        sis[sp][:], banks[2][:], AF.Sigmoid, bias=bia[:, 16 + oc:17 + oc]),
                        reads=[btok[2], "bia"], writes=[f"sis{sp}"])
                    asl = Atp[par][:, oc * TN:(oc + 1) * TN]
                    bsl = Btp[par][:, oc * TN:(oc + 1) * TN]
                    sch.add("vector", lambda v, sp=sp, asl=asl, oc=oc: v.tensor_scalar(
                        asl, srs[sp][:], a_sb[:, oc:oc + 1], None, OP.mult, OP.bypass),
                        reads=[f"srs{sp}", "a"], writes=[f"A:{par}:{oc}"])
                    sch.add("vector", lambda v, sp=sp, oc=oc: v.tensor_scalar(
                        srs[sp][:], srs[sp][:], na_sb[:, oc:oc + 1], 1.0, OP.mult, OP.add),
                        reads=[f"srs{sp}", "na"], writes=[f"srs{sp}"])
                    sch.add("vector", lambda v, sp=sp: v.tensor_tensor(
                        sis[sp][:], sis[sp][:], sin_[sp][:], OP.mult),
                        reads=[f"sis{sp}", f"sin{sp}"], writes=[f"sis{sp}"])
                    sch.add("vector", lambda v, sp=sp, bsl=bsl, oc=oc: v.tensor_tensor(
                        bsl, srs[sp][:], sis[sp][:], OP.mult),
                        reads=[f"srs{sp}", f"sis{sp}"], writes=[f"B:{par}:{oc}"])
                    if scan_tt is not None and oc < 4:
                        emit_scan(scan_tt, only_fc=2 * oc)
                        emit_scan(scan_tt, only_fc=2 * oc + 1)
                    if wout_tt is not None and oc >= 4:
                        emit_wout_chain(wout_tt, 2 * (oc - 4))
                        emit_wout_chain(wout_tt, 2 * (oc - 4) + 1)

            def emit_scan(tt, only_fc=None):
                par = tt % 2
                fcs = range(NKC) if only_fc is None else [only_fc]
                for fc in fcs:
                    hsl = Htp[par][:, fc * TN:(fc + 1) * TN]
                    asl = Atp[par][:, fc * TN:(fc + 1) * TN]
                    bsl = Btp[par][:, fc * TN:(fc + 1) * TN]
                    init = 0.0 if tt == 0 else carry[:, fc:fc + 1]
                    sch.add("vector", lambda v, hsl=hsl, asl=asl, bsl=bsl, init=init: v.tensor_tensor_scan(
                        hsl, asl, bsl, init, OP.mult, OP.add),
                        reads=[f"A:{par}:{fc}", f"B:{par}:{fc}"] + ([] if tt == 0 else [f"cr:{fc}"]),
                        writes=[f"H:{par}:{fc}"])
                    sch.add("vector", lambda v, fc=fc, par=par: v.tensor_copy(
                        carry[:, fc:fc + 1], Htp[par][:, fc * TN + TN - 1: fc * TN + TN]),
                        reads=[f"H:{par}:{fc}"], writes=[f"cr:{fc}"])

            def emit_wout_chain(tt, dc, ybanks=None):
                wpar = dc % 2
                sch.add("gpsimd", lambda g, dc=dc, wpar=wpar: g.dma_start(
                    out=wor[wpar][:].rearrange("p (k m) -> p k m", k=NKC),
                    in_=woh[dc]),
                    writes=[f"wor{wpar}"], dma_key=f"wo{wpar}")
                if ybanks is None:
                    pb = PC if dc % 2 == 0 else PD
                    pbt = "PC" if dc % 2 == 0 else "PD"
                else:
                    pb, pbt = ybanks[dc % len(ybanks)]
                par = tt % 2
                for ec in range(NKC):
                    sch.add("tensor", lambda t, wpar=wpar, ec=ec, pb=pb, par=par: t.matmul(
                        pb[:], wor[wpar][:, ec * 128:(ec + 1) * 128],
                        Htp[par][:, ec * TN:(ec + 1) * TN], start=(ec == 0), stop=(ec == NKC - 1)),
                        reads=[f"wor{wpar}", f"H:{par}:{ec}"], writes=[pbt])
                sch.add("vector", lambda v, dc=dc, tt=tt, pb=pb: v.scalar_tensor_tensor(
                    xs(dc, tt), pb[:], bia[:, 24 + dc:25 + dc], xs(dc, tt),
                    OP.add, OP.add),
                    reads=[pbt, "bia", f"X:{dc}:{tt}"], writes=[f"X:{dc}:{tt}"])

            def emit_wout(tt):
                for dc in range(NKC):
                    emit_wout_chain(tt, dc)

            if emit_attn_sublayer:
                emit_proj(0)
                for tt in range(NTT):
                    if tt + 2 < NTT:
                        emit_ln2(tt + 2)
                    if tt + 1 < NTT:
                        emit_proj(tt + 1, scan_tt=tt, wout_tt=tt)
                    elif stages == 2:
                        emit_scan(tt)
                        emit_wout(tt)

        if stages >= 3:
            sch.barrier()
            emit_scan(3)
            WB = [(PH[0], "PH0"), (PH[1], "PH1"), (PH[2], "PH2"), (PH[3], "PH3")]

            def _ln3(tt):
                emit_ln(3, tt,
                        lambda kc, tt=tt: (xns(kc, tt), f"XN:{kc}:{tt}:c3"),
                        "c3",
                        dest3_fn=lambda tt=tt: XN[:].rearrange(
                            "p (k s) -> p k s", k=NKC)[:, :, tt * TN:(tt + 1) * TN])
            _ln3(0)
            for dc in (0, 1, 2):
                emit_wout_chain(3, dc, ybanks=WB)
            _ln3(1)
            for dc in (3, 4, 5):
                emit_wout_chain(3, dc, ybanks=WB)
            _ln3(2)
            for dc in (6, 7):
                emit_wout_chain(3, dc, ybanks=WB)
            _ln3(3)
            emit_router(3)

        with contextlib.ExitStack() as ffn2_scope:
            fent = ffn2_scope.enter_context
            w1r = [fent(nc.sbuf_tensor(f"w1r{i}_b", [128, NKC * 128], BF16)) for i in range(4)]
            w2r = [fent(nc.sbuf_tensor(f"w2r{i}_b", [128, GS * 128], BF16)) for i in range(2)]
            rsc = [fent(nc.sbuf_tensor(f"rsc{i}_b", [128, TN], BF16)) for i in range(4)]
            if stages >= 3:
                emit_ffn_call(3, with_ln_router=False)

        for ti in range(NTI):
            p = ti % 3
            for dc in range(NKC):
                pt = PC if dc < 4 else PD
                ptok = "PC" if dc < 4 else "PD"
                src = X[:, dc * SEQ + ti * 128: dc * SEQ + (ti + 1) * 128]
                sch.add("tensor", lambda t, pt=pt, src=src, dc=dc: t.transpose(
                    pt[:, (dc % 4) * 128:(dc % 4 + 1) * 128], src, idn[:]),
                    reads=[f"X:{dc}:{ti // 4}", "idn"], writes=[ptok])
            for half, (pt, ptok) in enumerate([(PC, "PC"), (PD, "PD")]):
                dst = SIO[p][:, half * TN:(half + 1) * TN]
                sch.add("scalar", lambda s, dst=dst, pt=pt: s.copy(dst, pt[:]),
                        reads=[ptok],
                        writes=[f"SIO{p}"])
            sch.add("sync", lambda sy, p=p, ti=ti: sy.dma_start(
                out=outd[ti * 128:(ti + 1) * 128, :], in_=SIO[p][:]),
                reads=[f"SIO{p}"], writes=[f"OD:{ti}"], dma_key=f"sio{p}")
        sch.add("sync", lambda sy: sy.nop(),
                extra_deps=[sch.tok_w[f"OD:{ti}"]["sync"] for ti in range(NTI)],
                reads=[f"OD:{ti}" for ti in range(NTI)]
                + (["DBGOUT"] if debug else []))

        with nc.Block() as block:
            sch.emit(block, eng_sems, dma_sems)

    return nc


_NC_CACHE = None
_LAST_RES = None


def kernel(L0_attn_x, L0_attn_W_in, L0_attn_b_in, L0_attn_W_r, L0_attn_b_r,
           L0_attn_W_i, L0_attn_b_i, L0_attn_W_out, L0_attn_b_out, L0_attn_log_a,
           L0_ffn_router_w, L0_ffn_w1, L0_ffn_w2):
    x = np.ascontiguousarray(np.asarray(L0_attn_x, np.float32))

    def prep_sq(W):
        W = np.asarray(W, np.float32)
        return np.ascontiguousarray(W.reshape(8, 128, 8, 128).transpose(2, 1, 0, 3))

    w1hh = np.ascontiguousarray(
        np.asarray(L0_ffn_w1, np.float32).reshape(8, 128, 96, 128).transpose(2, 1, 0, 3))
    w2hh = np.ascontiguousarray(
        np.asarray(L0_ffn_w2, np.float32).reshape(12, 8, 128, 8, 128).transpose(3, 0, 2, 1, 4))
    wih = prep_sq(L0_attn_W_in)
    wrh = prep_sq(L0_attn_W_r)
    wxh = prep_sq(L0_attn_W_i)
    woh = prep_sq(L0_attn_W_out)
    rt = np.ascontiguousarray(
        np.asarray(L0_ffn_router_w, np.float32).T.reshape(8, 128, 8).transpose(1, 0, 2))

    def col(v):
        return np.asarray(v, np.float32).reshape(8, 128).T

    bcol = np.ascontiguousarray(np.concatenate(
        [col(L0_attn_b_in), col(L0_attn_b_r), col(L0_attn_b_i),
         col(L0_attn_b_out), col(L0_attn_log_a)], axis=1))
    sel = np.zeros((8, 1024), np.float32)
    for e in range(8):
        sel[e, e * 128:(e + 1) * 128] = 1.0
    onescol = np.ones((128, 1), np.float32)
    onesrow = np.ones((1, 128), np.float32)
    ident = np.eye(128, dtype=np.float32)
    epsv = np.full((128, 1), 1e-5, np.float32)

    stages = int(os.environ.get("MK_STAGES", "3"))
    dbgtag = os.environ.get("MK_DEBUG") or None
    global _NC_CACHE
    key = (stages, dbgtag)
    if _NC_CACHE is None or _NC_CACHE[0] != key:
        _NC_CACHE = (key, build_nc(stages, dbgtag))
    nc = _NC_CACHE[1]

    shared = {"w1h": w1hh, "w2h": w2hh, "wih": wih, "wrh": wrh, "wxh": wxh,
              "woh": woh, "rt": rt, "bcol": bcol, "sel": sel,
              "onescol": onescol, "onesrow": onesrow, "ident": ident, "epsd": epsv}
    in_maps = [dict(shared, x=np.ascontiguousarray(x[c])) for c in range(8)]
    res = run_bass_kernel_spmd(nc, in_maps, core_ids=list(range(8)))
    global _LAST_RES
    _LAST_RES = res
    out = np.stack([np.asarray(r["out"]) for r in res.results], axis=0)
    return out
